# version 15
# baseline (speedup 1.0000x reference)
import numpy as np
from contextlib import ExitStack

import concourse.bass as bass
import concourse.mybir as mybir
from concourse.bass_utils import run_bass_kernel_spmd

F32 = mybir.dt.float32
BF16 = mybir.dt.bfloat16
I32 = mybir.dt.int32
AF = mybir.ActivationFunctionType
ALU = mybir.AluOpType
AX = mybir.AxisListType

D = 1024
HID = 3072
NCORE = 8
EPS = 1e-5
DEN_EPS = 1e-6
QSCALE = 1.0 / 127.0


class Prog:

    ENGINES = ("sync", "gpsimd", "scalar", "vector", "tensor")

    def __init__(self, nc, ctx):
        self.nc = nc
        self.cnt = {}
        self.h = {}
        for name in ("mm", "dve", "act", "pool", "da", "db", "dta", "dtb", "dw",
                     "dwp", "cwp", "cwg", "cwu", "cwd", "lwg", "lwu", "lwd",
                     "sq", "sk", "skp", "sv", "sx1", "sxn0", "sxn1",
                     "sh0", "sh1", "so"):
            self.h[name] = ctx.enter_context(nc.semaphore(name))
            self.cnt[name] = 0
        self.track = {}
        self._reset()

    def _reset(self):
        self.ops = {e: [] for e in self.ENGINES}
        self.lastwait = {}

    def inc(self, name, amount=1):
        self.cnt[name] += amount
        return (name, self.cnt[name])

    def wait(self, eng, ev):
        if ev is None:
            return
        name, val = ev
        if val <= 0:
            return
        key = (eng, name)
        if self.lastwait.get(key, -1) >= val:
            return
        self.lastwait[key] = val
        h = self.h[name]
        self.ops[eng].append(lambda e, h=h, v=val: e.wait_ge(h, v))

    def waits(self, eng, evs):
        for ev in evs:
            self.wait(eng, ev)

    def emit(self, eng, fn, sem=None, amount=1, reads=(), writes=(), waits=()):
        for ev in waits:
            self.wait(eng, ev)
        for t in reads:
            rec = self.track.get(t)
            if rec is not None and rec[0] is not None:
                self.wait(eng, rec[0][0])
        for t in writes:
            rec = self.track.get(t)
            if rec is not None:
                w = rec[0]
                if w is not None and not (eng == "tensor" and w[1] == "tensor"
                                          and t.startswith("ps:")):
                    self.wait(eng, w[0])
                for rev, _ in rec[1]:
                    self.wait(eng, rev)
        if sem is not None:
            ev = self.inc(sem, amount)
            h = self.h[sem]
            self.ops[eng].append(lambda e, fn=fn, h=h, a=amount: fn(e).then_inc(h, a))
        else:
            assert not reads and not writes, "tracked ops need a sem"
            self.ops[eng].append(fn)
            return None
        for t in reads:
            self.track.setdefault(t, [None, []])[1].append((ev, eng))
        for t in writes:
            self.track[t] = [(ev, eng), []]
        return ev

    def snapshot(self):
        return dict(self.cnt)

    def barrier(self, snap):
        for eng in self.ENGINES:
            for name, val in snap.items():
                self.wait(eng, (name, val))

    def run_block(self):
        with self.nc.Block() as block:
            for eng in self.ENGINES:
                ops = self.ops[eng]
                getattr(block, eng)(lambda e, ops=ops: [op(e) for op in ops])
        self._reset()


def build_nc(T=4096, phases=(1, 2, 4, 5), dev_scratch=False):
    nc = bass.Bass("TRN2", target_bir_lowering=False)
    NCH = T // 128
    SC = 512
    CPS = SC // 128
    NSC = T // SC
    NT5 = T // 512

    x_d = nc.dram_tensor("x", [T, D], F32, kind="ExternalInput").ap()
    w_qkv_d = nc.dram_tensor("w_qkv", [D, 3 * D], F32, kind="ExternalInput").ap()
    w_gate_d = nc.dram_tensor("w_gate", [D, D], F32, kind="ExternalInput").ap()
    w_proj_d = nc.dram_tensor("w_proj", [D, D], F32, kind="ExternalInput").ap()
    b_qkv_d = nc.dram_tensor("b_qkv_fm", [128, 24], F32, kind="ExternalInput").ap()
    b_gate_d = nc.dram_tensor("b_gate_fm", [128, 8], F32, kind="ExternalInput").ap()
    b_proj_d = nc.dram_tensor("b_proj_t", [128, D], F32, kind="ExternalInput").ap()
    b_v_d = nc.dram_tensor("b_v_t", [128, D], F32, kind="ExternalInput").ap()
    ln_g_d = nc.dram_tensor("ln_g_fm", [128, 8], F32, kind="ExternalInput").ap()
    ln_b_d = nc.dram_tensor("ln_b_fm", [128, 8], F32, kind="ExternalInput").ap()
    wg_d = nc.dram_tensor("wg_t", [D, HID], I32, kind="ExternalInput").ap()
    wu_d = nc.dram_tensor("wu_t", [D, HID], I32, kind="ExternalInput").ap()
    wd_d = nc.dram_tensor("wd_t", [HID, D], I32, kind="ExternalInput").ap()
    out_d = nc.dram_tensor("out", [T, D], F32, kind="ExternalOutput").ap()

    skind = "ExternalOutput" if dev_scratch else "Internal"
    if dev_scratch:
        dbg_xn = nc.dram_tensor("dbg_xn", [NSC, 128, 8 * SC], BF16, kind=skind).ap()
        dbg_gate = nc.dram_tensor("dbg_gate", [NSC, 128, SC], F32, kind=skind).ap()
        dbg_xntok = nc.dram_tensor("dbg_xntok", [NSC, 128, 4 * 1024], BF16, kind=skind).ap()
        dbg_ident = nc.dram_tensor("dbg_ident", [128, 128], BF16, kind=skind).ap()
        dbg_stats = nc.dram_tensor("dbg_stats", [NSC, 128, 128], F32, kind=skind).ap()
    q_sc = nc.dram_tensor("q_sc", [NSC, 128, 8 * SC], BF16, kind=skind).ap()
    k_sc = nc.dram_tensor("k_sc", [NSC, 128, 8 * SC], BF16, kind=skind).ap()
    k_pl = nc.dram_tensor("k_pl", [D, T], BF16, kind=skind).ap()
    v_pl = nc.dram_tensor("v_pl", [T, D], BF16, kind=skind).ap()
    x1_sc = nc.dram_tensor("x1_sc", [T, D], F32, kind=skind).ap()
    xn4_sc = nc.dram_tensor("xn4_sc", [T, D], BF16, kind=skind).ap()
    h_sc = nc.dram_tensor("h_sc", [NT5, 128, 24 * 512], BF16, kind=skind).ap()
    wp_bf = nc.dram_tensor("wp_bf", [D, D], BF16, kind="Internal").ap()
    wg_bf = nc.dram_tensor("wg_bf", [D, HID], BF16, kind="Internal").ap()
    wu_bf = nc.dram_tensor("wu_bf", [D, HID], BF16, kind="Internal").ap()
    wd_bf = nc.dram_tensor("wd_bf", [HID, D], BF16, kind="Internal").ap()

    ctx = ExitStack()
    with ctx:
        P = Prog(nc, ctx)

        def sb(name, shape, dtype):
            return ctx.enter_context(nc.sbuf_tensor(name, shape, dtype))

        def ps(name, shape, dtype=F32):
            return ctx.enter_context(nc.psum_tensor(name, shape, dtype))

        ident = sb("ident", [128, 128], BF16)
        mask_keep = sb("mask_keep", [128, 128], BF16)
        b_qkv = sb("c_b_qkv", [128, 24], F32)
        b_gate = sb("c_b_gate", [128, 8], F32)
        nb_gate = sb("nb_gate", [128, 8], F32)
        b_proj_t = sb("c_b_proj", [128, D], F32)
        b_v_t = sb("c_b_v", [128, D], F32)
        ln_g = sb("c_ln_g", [128, 8], F32)
        ln_b = sb("c_ln_b", [128, 8], F32)
        sq_dump = sb("sq_dump", [128, 1024], F32)
        stats = sb("stats", [128, 128], F32)

        ev_consts = None
        for dst, src in ((b_qkv, b_qkv_d), (b_gate, b_gate_d), (b_proj_t, b_proj_d),
                         (b_v_t, b_v_d), (ln_g, ln_g_d), (ln_b, ln_b_d)):
            ev_consts = P.emit("sync", lambda e, d=dst, s=src: e.dma_start(out=d[:], in_=s),
                               sem="dw", amount=16)
        P.wait("vector", ev_consts)
        P.wait("scalar", ev_consts)
        ev_nbg = P.emit("vector", lambda e: e.tensor_scalar(
            nb_gate[:], b_gate[:], -1.0, None, ALU.mult), sem="dve")
        ev0 = P.emit("gpsimd", lambda e: e.memset(ident[:], 0.0), sem="pool")
        P.wait("gpsimd", ev0)
        ev_ident = P.emit("gpsimd", lambda e: e.affine_select(
            out=ident[:], in_=ident[:], compare_op=ALU.not_equal, fill=1.0,
            base=0, pattern=[[-1, 128]], channel_multiplier=1), sem="pool")
        ev0 = P.emit("gpsimd", lambda e: e.memset(mask_keep[:], 1.0), sem="pool")
        P.wait("gpsimd", ev0)
        ev_mask = P.emit("gpsimd", lambda e: e.affine_select(
            out=mask_keep[:], in_=mask_keep[:], compare_op=ALU.is_ge, fill=0.0,
            base=0, pattern=[[1, 128]], channel_multiplier=-1), sem="pool")

        st_ss = stats[:, 0:16]
        st_sum = stats[:, 16:32]
        st_m = stats[:, 32:48]
        st_v = stats[:, 48:64]
        st_r = stats[:, 64:80]
        st_nmr = stats[:, 80:96]
        st_tmp = stats[:, 96:112]
        st_den = stats[:, 112:116]
        st_rc = stats[:, 116:120]

        def ln_stats(col, x_ap, ev_x, scale_var, bias_var, xtok=None):
            c = col
            xr = (xtok,) if xtok else ()
            ev_s = P.emit("vector", lambda e: e.reduce_sum(
                st_sum[:, c:c + 1], x_ap, axis=AX.X), sem="dve",
                waits=[ev_x], reads=xr, writes=(f"su{c}",))
            ev_q = P.emit("scalar", lambda e: e.activation(
                sq_dump[:], x_ap, AF.Square, accum_out=st_ss[:, c:c + 1]), sem="act",
                waits=[ev_x], reads=xr, writes=("sqd", f"ss{c}"))
            ev_m = P.emit("vector", lambda e: e.tensor_scalar(
                st_m[:, c:c + 1], st_sum[:, c:c + 1], 1.0 / D, None, ALU.mult),
                sem="dve", reads=(f"su{c}",), writes=(f"m{c}",))
            P.emit("vector", lambda e: e.scalar_tensor_tensor(
                st_tmp[:, c:c + 1], st_m[:, c:c + 1], scale_var, st_m[:, c:c + 1],
                ALU.mult, ALU.mult), sem="dve", reads=(f"m{c}",), writes=(f"tp{c}",))
            P.emit("vector", lambda e: e.tensor_scalar(
                st_v[:, c:c + 1], st_ss[:, c:c + 1], scale_var / D, bias_var,
                ALU.mult, ALU.add), sem="dve", reads=(f"ss{c}",), writes=(f"v{c}",))
            ev_v = P.emit("vector", lambda e: e.tensor_sub(
                st_v[:, c:c + 1], st_v[:, c:c + 1], st_tmp[:, c:c + 1]), sem="dve",
                reads=(f"v{c}", f"tp{c}"), writes=(f"v{c}",))
            P.emit("scalar", lambda e: e.activation(
                st_tmp[:, c:c + 1], st_v[:, c:c + 1], AF.Ln), sem="act",
                reads=(f"v{c}",), writes=(f"tp{c}",))
            ev_r = P.emit("scalar", lambda e: e.activation(
                st_r[:, c:c + 1], st_tmp[:, c:c + 1], AF.Exp, scale=-0.5), sem="act",
                reads=(f"tp{c}",), writes=(f"r{c}",))
            ev_n = P.emit("vector", lambda e: e.scalar_tensor_tensor(
                st_nmr[:, c:c + 1], st_m[:, c:c + 1], -1.0, st_r[:, c:c + 1],
                ALU.mult, ALU.mult), sem="dve",
                reads=(f"m{c}", f"r{c}"), writes=(f"n{c}",))
            return st_r[:, c:c + 1], st_nmr[:, c:c + 1], ev_n, ev_r

        if 1 in phases:
            with ExitStack() as p1:
                def sb1(name, shape, dtype):
                    return p1.enter_context(nc.sbuf_tensor(name, shape, dtype))
                w_qkv_b = sb1("w_qkv_b", [128, 8 * 3072], BF16)
                w_gate_b = sb1("w_gate_b", [128, 8 * 1024], BF16)
                xb = [sb1("xb0", [128, CPS * 1024], F32),
                      sb1("xb1", [128, CPS * 1024], F32)]
                xn_tok = sb1("xn_tok", [128, CPS * 1024], BF16)
                xn_fm = sb1("xn_fm", [128, 8 * SC], BF16)
                q_fm = sb1("q_fm", [128, 8 * SC], BF16)
                k_fm = sb1("k_fm", [128, 8 * SC], BF16)
                v_tk = sb1("v_tk", [128, CPS * 1024], BF16)
                scr = [sb1(f"scr{i}", [128, SC], F32) for i in range(10)]
                pm = [p1.enter_context(nc.psum_tensor(f"pm{i}", [128, 512], F32))
                      for i in range(4)]
                pT = p1.enter_context(nc.psum_tensor("pT", [128, 2048], BF16))

                ev_w1 = None
                for kk in range(8):
                    ev_w1 = P.emit("gpsimd", lambda e, kk=kk: e.dma_start(
                        out=w_qkv_b[:, kk * 3072:(kk + 1) * 3072],
                        in_=w_qkv_d[kk * 128:(kk + 1) * 128, :]), sem="dwp", amount=16)
                for kk in range(8):
                    ev_w1 = P.emit("gpsimd", lambda e, kk=kk: e.dma_start(
                        out=w_gate_b[:, kk * 1024:(kk + 1) * 1024],
                        in_=w_gate_d[kk * 128:(kk + 1) * 128, :]), sem="dwp", amount=16)
                ev_cast_wp = P.emit("gpsimd", lambda e: e.dma_start(
                    out=wp_bf, in_=w_proj_d), sem="cwp", amount=16)
                ev_cast_wg = P.emit("gpsimd", lambda e: e.dma_start(
                    out=wg_bf, in_=wg_d), sem="cwg", amount=16)
                ev_cast_wu = P.emit("gpsimd", lambda e: e.dma_start(
                    out=wu_bf, in_=wu_d), sem="cwu", amount=16)
                ev_cast_wd = P.emit("gpsimd", lambda e: e.dma_start(
                    out=wd_bf, in_=wd_d), sem="cwd", amount=16)

                ev_x = [None, None]
                ev_xb_free = [None, None]
                ev_pm_free = [None, None, None, None]
                ev_pT_free = [None, None]
                ev_mm_last = None
                ev_store = {"q": None, "k": None, "kp": None, "v": None}

                def load_x(g):
                    par = g % 2
                    evs = []
                    if ev_xb_free[par] is not None:
                        P.waits("sync", ev_xb_free[par])
                    for c in range(CPS):
                        evs.append(P.emit("sync", lambda e, g=g, c=c, par=par:
                                          e.dma_start(out=xb[par][:, c * 1024:(c + 1) * 1024],
                                                      in_=x_d[g * SC + c * 128:
                                                              g * SC + (c + 1) * 128, :]),
                                          sem="da" if par == 0 else "db", amount=16))
                    ev_x[par] = evs[-1]

                def emit_ln12(g):
                    par = g % 2
                    xcur = xb[par]
                    sv = 1.0 + EPS
                    bv = EPS * EPS
                    ev_x_g = ev_x[par]
                    xc = [xcur[:, c * 1024:(c + 1) * 1024] for c in range(CPS)]
                    for c in range(CPS):
                        P.emit("vector", lambda e, c=c: e.reduce_sum(
                            st_sum[:, c:c + 1], xc[c], axis=AX.X), sem="dve",
                            waits=[ev_x_g], writes=(f"su{c}",))
                    for c in range(CPS):
                        P.emit("scalar", lambda e, c=c: e.activation(
                            sq_dump[:], xc[c], AF.Square,
                            accum_out=st_ss[:, c:c + 1]), sem="act",
                            waits=[ev_x_g], writes=("sqd", f"ss{c}"))
                    for c in range(CPS):
                        P.emit("vector", lambda e, c=c: e.tensor_scalar(
                            st_m[:, c:c + 1], st_sum[:, c:c + 1], 1.0 / D, None,
                            ALU.mult), sem="dve", reads=(f"su{c}",),
                            writes=(f"m{c}",))
                    for c in range(CPS):
                        P.emit("vector", lambda e, c=c: e.scalar_tensor_tensor(
                            st_tmp[:, c:c + 1], st_m[:, c:c + 1], sv,
                            st_m[:, c:c + 1], ALU.mult, ALU.mult), sem="dve",
                            reads=(f"m{c}",), writes=(f"tp{c}",))
                    for c in range(CPS):
                        P.emit("vector", lambda e, c=c: e.tensor_scalar(
                            st_v[:, c:c + 1], st_ss[:, c:c + 1], sv / D, bv,
                            ALU.mult, ALU.add), sem="dve", reads=(f"ss{c}",),
                            writes=(f"v{c}",))
                    for c in range(CPS):
                        P.emit("vector", lambda e, c=c: e.tensor_sub(
                            st_v[:, c:c + 1], st_v[:, c:c + 1],
                            st_tmp[:, c:c + 1]), sem="dve",
                            reads=(f"v{c}", f"tp{c}"), writes=(f"v{c}",))
                    for c in range(CPS):
                        P.emit("scalar", lambda e, c=c: e.activation(
                            st_tmp[:, c:c + 1], st_v[:, c:c + 1], AF.Ln),
                            sem="act", reads=(f"v{c}",), writes=(f"tp{c}",))
                    for c in range(CPS):
                        P.emit("scalar", lambda e, c=c: e.activation(
                            st_r[:, c:c + 1], st_tmp[:, c:c + 1], AF.Exp,
                            scale=-0.5), sem="act", reads=(f"tp{c}",),
                            writes=(f"r{c}",))
                    ev_n = None
                    for c in range(CPS):
                        ev_n = P.emit("vector", lambda e, c=c: e.scalar_tensor_tensor(
                            st_nmr[:, c:c + 1], st_m[:, c:c + 1], -1.0,
                            st_r[:, c:c + 1], ALU.mult, ALU.mult), sem="dve",
                            reads=(f"m{c}", f"r{c}"), writes=(f"n{c}",))
                    evn = []
                    for c in range(CPS):
                        ev = P.emit("scalar", lambda e, c=c: e.activation(
                            xn_tok[:, c * 1024:(c + 1) * 1024],
                            xc[c], AF.Identity, bias=st_nmr[:, c:c + 1],
                            scale=st_r[:, c:c + 1]), sem="act",
                            waits=[ev_pT_free[0], ev_pT_free[1]],
                            reads=(f"r{c}", f"n{c}"))
                        evn.append(ev)
                    ev_xb_free[par] = [evn[-1], ev_n]
                    return evn

                load_x(0)
                evn = emit_ln12(0)
                for g in range(NSC):
                    par = g % 2
                    if g + 1 < NSC:
                        load_x(g + 1)
                    ev_xn = []
                    for j in range(8):
                        tpar = j % 2
                        P.wait("tensor", evn[-1])
                        P.wait("tensor", ev_ident)
                        P.wait("tensor", ev_pT_free[tpar])
                        P.wait("tensor", ev_mm_last)
                        evt = None
                        for c in range(CPS):
                            evt = P.emit("tensor", lambda e, c=c, j=j, tpar=tpar:
                                         e.transpose(pT[:, tpar * 1024 + c * 128:
                                                        tpar * 1024 + (c + 1) * 128],
                                                     xn_tok[:, c * 1024 + j * 128:
                                                            c * 1024 + (j + 1) * 128],
                                                     ident[:]), sem="mm")
                        P.wait("scalar", evt)
                        ev = P.emit("scalar", lambda e, j=j, tpar=tpar:
                                    e.activation(xn_fm[:, j * SC:(j + 1) * SC],
                                                 pT[:, tpar * 1024:tpar * 1024 + 512],
                                                 AF.Identity, bias=ln_b[:, j:j + 1],
                                                 scale=ln_g[:, j:j + 1]), sem="act")
                        ev_pT_free[tpar] = ev
                        ev_xn.append(ev)
                    slot_rr = [0]

                    def mm8(wsb, col0, extra_waits=()):
                        psl = slot_rr[0]
                        slot_rr[0] = (psl + 1) % 4
                        P.wait("tensor", ev_xn[-1])
                        P.wait("tensor", ev_w1)
                        P.wait("tensor", ev_pm_free[psl])
                        for w in extra_waits:
                            P.wait("tensor", w)
                        ev = None
                        for kk in range(8):
                            ev = P.emit("tensor", lambda e, psl=psl, wsb=wsb, kk=kk, col0=col0:
                                        e.matmul(pm[psl][:],
                                                 wsb[:, kk * (wsb.shape[1] // 8) + col0:
                                                     kk * (wsb.shape[1] // 8) + col0 + 128],
                                                 xn_fm[:, kk * SC:(kk + 1) * SC],
                                                 start=(kk == 0), stop=(kk == 7)),
                                        sem="mm")
                        return psl, ev

                    pend_q = []

                    def emit_gate_k(j):
                        sA, sB, sC, sD, sE = (scr[(j % 2) * 5 + i] for i in range(5))
                        tA, tB, tC, tD, tE = (f"scr{(j % 2) * 5 + i}" for i in range(5))
                        gsl, evg = mm8(w_gate_b, j * 128)
                        ev_e1 = P.emit("scalar", lambda e, sA=sA, j=j, gsl=gsl:
                                       e.activation(sA[:], pm[gsl][:], AF.Exp,
                                                    scale=-1.0,
                                                    bias=nb_gate[:, j:j + 1]),
                                       sem="act", waits=[evg, ev_nbg], writes=(tA,))
                        ev_pm_free[gsl] = ev_e1
                        P.emit("scalar", lambda e, sA=sA, sB=sB:
                               e.activation(sB[:], sA[:], AF.Ln, bias=1.0),
                               sem="act", reads=(tA,), writes=(tB,))
                        ev_gate = P.emit("scalar", lambda e, sB=sB, sA=sA: e.activation(
                            sA[:], sB[:], AF.Exp, scale=-1.0), sem="act",
                            reads=(tB,), writes=(tA,))
                        ksl, evk = mm8(w_qkv_b, 1024 + j * 128)
                        ev_stt = P.emit("vector", lambda e, sC=sC, j=j, sA=sA, ksl=ksl:
                                        e.scalar_tensor_tensor(
                                            sC[:], pm[ksl][:], b_qkv[:, 8 + j:9 + j],
                                            sA[:], ALU.add, ALU.mult), sem="dve",
                                        waits=[evk], reads=(tA,), writes=(tC,))
                        ev_pm_free[ksl] = ev_stt
                        P.emit("vector", lambda e, sC=sC, sD=sD:
                               e.tensor_scalar_max(sD[:], sC[:], 0.0), sem="dve",
                               reads=(tC,), writes=(tD,))
                        P.emit("vector", lambda e, sC=sC: e.tensor_scalar_min(
                            sC[:], sC[:], 0.0), sem="dve", reads=(tC,), writes=(tC,))
                        P.emit("scalar", lambda e, sC=sC, sE=sE: e.activation(
                            sE[:], sC[:], AF.Exp), sem="act",
                            reads=(tC,), writes=(tE,))
                        ev_kj = P.emit("vector", lambda e, sD=sD, sE=sE, j=j:
                                       e.tensor_add(k_fm[:, j * SC:(j + 1) * SC],
                                                    sD[:], sE[:]), sem="dve",
                                       waits=[ev_store["k"], ev_store["kp"]],
                                       reads=(tD, tE))
                        return ev_kj

                    def emit_q(j):
                        sA, sB, sC, sD, sE = (scr[(j % 2) * 5 + i] for i in range(5))
                        tA, tB, tC, tD, tE = (f"scr{(j % 2) * 5 + i}" for i in range(5))
                        qsl, evq = mm8(w_qkv_b, j * 128)
                        P.emit("vector", lambda e, sD=sD, j=j, qsl=qsl: e.tensor_scalar(
                            sD[:], pm[qsl][:], b_qkv[:, j:j + 1], 0.0, ALU.add,
                            ALU.max), sem="dve", waits=[evq], writes=(tD,))
                        ev_pm_free[qsl] = P.emit(
                            "vector", lambda e, sC=sC, j=j, qsl=qsl:
                            e.tensor_scalar(sC[:], pm[qsl][:], b_qkv[:, j:j + 1],
                                            0.0, ALU.add, ALU.min), sem="dve",
                            writes=(tC,))
                        P.emit("scalar", lambda e, sC=sC, sE=sE: e.activation(
                            sE[:], sC[:], AF.Exp), sem="act",
                            reads=(tC,), writes=(tE,))
                        ev_qj = P.emit("vector", lambda e, sD=sD, sE=sE, j=j:
                                       e.tensor_add(q_fm[:, j * SC:(j + 1) * SC],
                                                    sD[:], sE[:]), sem="dve",
                                       waits=[ev_store["q"]], reads=(tD, tE))
                        return ev_qj

                    ev_kj = ev_qj = None
                    for j in range(8):
                        ev_kj = emit_gate_k(j)
                        if j >= 1:
                            ev_qj = emit_q(j - 1)
                        if j == 5 and g + 1 < NSC:
                            evn = emit_ln12(g + 1)
                        ev_gate_dbg = None
                    ev_qj = emit_q(7)
                    ev_vv = []
                    for c in range(CPS):
                        s0 = slot_rr[0]; slot_rr[0] = (s0 + 1) % 4
                        s1 = slot_rr[0]; slot_rr[0] = (s1 + 1) % 4
                        P.waits("tensor", [ev_xn[-1], ev_w1, ev_pm_free[s0],
                                           ev_pm_free[s1]])
                        ev0 = ev1 = None
                        for kk in range(8):
                            ev0 = P.emit("tensor", lambda e, c=c, kk=kk, s0=s0: e.matmul(
                                pm[s0][:],
                                xn_fm[:, kk * SC + c * 128:kk * SC + (c + 1) * 128],
                                w_qkv_b[:, kk * 3072 + 2048:kk * 3072 + 2560],
                                start=(kk == 0), stop=(kk == 7)), sem="mm")
                            ev1 = P.emit("tensor", lambda e, c=c, kk=kk, s1=s1: e.matmul(
                                pm[s1][:],
                                xn_fm[:, kk * SC + c * 128:kk * SC + (c + 1) * 128],
                                w_qkv_b[:, kk * 3072 + 2560:kk * 3072 + 3072],
                                start=(kk == 0), stop=(kk == 7)), sem="mm")
                        P.wait("vector", ev1)
                        P.wait("vector", ev_store["v"])
                        ev_pm_free[s0] = P.emit("vector", lambda e, c=c, s0=s0:
                                                e.tensor_add(
                                                    v_tk[:, c * 1024:c * 1024 + 512],
                                                    pm[s0][:], b_v_t[:, 0:512]),
                                                sem="dve")
                        ev_pm_free[s1] = P.emit("vector", lambda e, c=c, s1=s1:
                                                e.tensor_add(
                                                    v_tk[:, c * 1024 + 512:
                                                         (c + 1) * 1024],
                                                    pm[s1][:], b_v_t[:, 512:1024]),
                                                sem="dve")
                        ev_vv.append(ev_pm_free[s1])
                    ev_mm_last = ev1
                    if dev_scratch:
                        P.waits("sync", [ev_xn[-1], ev_gate_dbg])
                        P.emit("sync", lambda e, g=g: e.dma_start(
                            out=dbg_xn[g], in_=xn_fm[:]), sem="so", amount=16)
                        P.emit("sync", lambda e, g=g: e.dma_start(
                            out=dbg_gate[g], in_=scr[5][:]), sem="so", amount=16)
                        P.emit("sync", lambda e, g=g: e.dma_start(
                            out=dbg_xntok[g], in_=xn_tok[:]), sem="so", amount=16)
                        P.emit("sync", lambda e, g=g: e.dma_start(
                            out=dbg_stats[g], in_=stats[:]), sem="so", amount=16)
                        if g == 0:
                            P.emit("sync", lambda e: e.dma_start(
                                out=dbg_ident, in_=ident[:]), sem="so", amount=16)
                    ev_q_all = ev_qj
                    ev_k_all = ev_kj
                    P.wait("sync", ev_q_all)
                    ev_store["q"] = P.emit("sync", lambda e, g=g: e.dma_start(
                        out=q_sc[g], in_=q_fm[:]), sem="sq", amount=16)
                    P.wait("sync", ev_k_all)
                    ev_store["k"] = P.emit("sync", lambda e, g=g: e.dma_start(
                        out=k_sc[g], in_=k_fm[:]), sem="sk", amount=16)
                    evkp = None
                    for j in range(8):
                        evkp = P.emit("sync", lambda e, g=g, j=j: e.dma_start(
                            out=k_pl[j * 128:(j + 1) * 128, g * SC:(g + 1) * SC],
                            in_=k_fm[:, j * SC:(j + 1) * SC]), sem="skp", amount=16)
                    ev_store["kp"] = evkp
                    evv = None
                    for c in range(CPS):
                        P.wait("sync", ev_vv[c])
                        evv = P.emit("sync", lambda e, g=g, c=c: e.dma_start(
                            out=v_pl[g * SC + c * 128:g * SC + (c + 1) * 128, :],
                            in_=v_tk[:, c * 1024:(c + 1) * 1024]), sem="sv", amount=16)
                    ev_store["v"] = evv
                p1_done = P.snapshot()
            P.run_block()

        if 2 in phases:
            with ExitStack() as p2:
                def sb2(name, shape, dtype):
                    return p2.enter_context(nc.sbuf_tensor(name, shape, dtype))
                S_f = sb2("S_f", [128, 8 * 1025], F32)
                S_b = sb2("S_b", [128, 8 * 1025], BF16)
                w_proj_b = sb2("w_proj_b", [128, 8 * 1024], BF16)
                q_sb = [sb2(f"q_sb{i}", [128, 8 * SC], BF16) for i in range(2)]
                k_sb = [sb2(f"k_sb{i}", [128, 8 * SC], BF16) for i in range(2)]
                kT_sb = [sb2(f"kT_sb{i}", [128, CPS * 1024], BF16) for i in range(2)]
                v_sb = [sb2(f"v_sb{i}", [128, CPS * 1025], BF16) for i in range(2)]
                x_sb = [sb2(f"x_sb{i}", [128, CPS * 1024], F32) for i in range(2)]
                aT_sb = sb2("aT_sb", [128, 2 * CPS * 128], BF16)
                odiv_tok = sb2("odiv_tok", [128, 1024], BF16)
                odiv_fm = sb2("odiv_fm", [128, 1024], BF16)
                ao_sb = sb2("ao_sb", [128, 1024], F32)
                t3_sb = sb2("t3_sb", [128, 1024], F32)
                xn4_tok = [sb2(f"xn4_{i}", [128, 1024], BF16) for i in range(2)]
                pA = p2.enter_context(nc.psum_tensor("pA", [128, 1280], F32))
                pO = p2.enter_context(nc.psum_tensor("pO", [128, 1280], F32))
                pT2 = p2.enter_context(nc.psum_tensor("pT2", [128, 2048], BF16))

                if 1 in phases:
                    P.barrier(p1_done)
                ev_Sf0 = P.emit("gpsimd", lambda e: e.memset(S_f[:], 0.0), sem="pool")
                ev_S0 = P.emit("gpsimd", lambda e: e.memset(S_b[:], 0.0), sem="pool")
                ev_vones = None
                for par in range(2):
                    for c in range(CPS):
                        ev_vones = P.emit("gpsimd", lambda e, par=par, c=c: e.memset(
                            v_sb[par][:, c * 1025 + 1024:(c + 1) * 1025], 1.0), sem="pool")
                ev_wp = None
                P.wait("sync", ev_cast_wp)
                for kk in range(8):
                    ev_wp = P.emit("sync", lambda e, kk=kk: e.dma_start(
                        out=w_proj_b[:, kk * 1024:(kk + 1) * 1024],
                        in_=wp_bf[kk * 128:(kk + 1) * 128, :]), sem="dw", amount=16)

                ev_ld = [None, None]
                ev_ldT = [None, None]
                ev_buf_free = [None, None]
                ev_S_cast = ev_S0
                ev_S_read = None
                ev_pA_free = [None, None]
                ev_pO_free = None
                ev_pT2_free = [None, None]
                ev_odiv_free = None
                ev_x1st = None
                ev_xn4st = [None, None]
                ev_aT_free = [[None] * CPS, [None] * CPS]
                ev_pairs = {}

                def load_sc(g):
                    par = g % 2
                    if ev_buf_free[par] is not None:
                        P.waits("sync", ev_buf_free[par])
                    evs = []
                    dsem = "da" if par == 0 else "db"
                    evs.append(P.emit("sync", lambda e, g=g, par=par: e.dma_start(
                        out=q_sb[par][:], in_=q_sc[g]), sem=dsem, amount=16))
                    evs.append(P.emit("sync", lambda e, g=g, par=par: e.dma_start(
                        out=k_sb[par][:], in_=k_sc[g]), sem=dsem, amount=16))
                    for c in range(CPS):
                        evs.append(P.emit("sync", lambda e, g=g, c=c, par=par: e.dma_start(
                            out=v_sb[par][:, c * 1025:c * 1025 + 1024],
                            in_=v_pl[g * SC + c * 128:g * SC + (c + 1) * 128, :]),
                            sem=dsem, amount=16))
                        evs.append(P.emit("sync", lambda e, g=g, c=c, par=par: e.dma_start(
                            out=x_sb[par][:, c * 1024:(c + 1) * 1024],
                            in_=x_d[g * SC + c * 128:g * SC + (c + 1) * 128, :]),
                            sem=dsem, amount=16))
                    ev_ld[par] = evs[-1]
                    if ev_buf_free[par] is not None:
                        P.waits("scalar", ev_buf_free[par])
                    evT = None
                    for c in range(CPS):
                        evT = P.emit("scalar", lambda e, g=g, c=c, par=par: e.dma_start(
                            out=kT_sb[par][:, c * 1024:(c + 1) * 1024],
                            in_=k_pl[:, (g * CPS + c) * 128:(g * CPS + c + 1) * 128],
                            transpose=True), sem="dta" if par == 0 else "dtb",
                            amount=16)
                    ev_ldT[par] = evT

                def emit_pairs(t):
                    g, c = t // CPS, t % CPS
                    par = g % 2
                    pp = (t % 2) * CPS * 128
                    ks, qs = k_sb[par], q_sb[par]
                    out = []
                    for cp in range(c + 1):
                        sl = cp % 2
                        off = 0 if sl == 0 else 512
                        P.waits("tensor", [ev_ld[par], ev_pA_free[sl]])
                        evt = None
                        for kk in range(8):
                            evt = P.emit("tensor",
                                         lambda e, kk=kk, cp=cp, c=c, off=off,
                                         ks=ks, qs=qs:
                                         e.matmul(pA[:, off:off + 128],
                                                  ks[:, kk * SC + cp * 128:
                                                     kk * SC + (cp + 1) * 128],
                                                  qs[:, kk * SC + c * 128:
                                                     kk * SC + (c + 1) * 128],
                                                  start=(kk == 0), stop=(kk == 7)),
                                         sem="mm" if kk == 7 else None)
                        if cp == c:
                            ev = P.emit("vector", lambda e, cp=cp, off=off, pp=pp:
                                        e.tensor_mul(
                                            aT_sb[:, pp + cp * 128:pp + (cp + 1) * 128],
                                            pA[:, off:off + 128],
                                            mask_keep[:]), sem="dve",
                                        waits=[evt, ev_mask,
                                               ev_aT_free[t % 2][cp]])
                        else:
                            ev = P.emit("scalar", lambda e, cp=cp, off=off, pp=pp:
                                        e.activation(
                                            aT_sb[:, pp + cp * 128:pp + (cp + 1) * 128],
                                            pA[:, off:off + 128],
                                            AF.Identity), sem="act",
                                        waits=[evt, ev_aT_free[t % 2][cp]])
                        ev_pA_free[sl] = ev
                        out.append(ev)
                    return out

                load_sc(0)
                ev_pairs[0] = emit_pairs(0)
                for g in range(NSC):
                    par = g % 2
                    if g + 1 < NSC:
                        load_sc(g + 1)
                    qs, ks, kTs, vs, xs = (q_sb[par], k_sb[par], kT_sb[par],
                                           v_sb[par], x_sb[par])
                    for c in range(CPS):
                        tglob = g * CPS + c
                        pp = (tglob % 2) * CPS * 128
                        evacs = ev_pairs.pop(tglob)
                        first = {0: True, 1: True, 2: True}
                        P.waits("tensor", [ev_pO_free, ev_ld[par], ev_vones])
                        ev_cross_last = None
                        if g > 0:
                            P.waits("tensor", [ev_S_cast])
                            for j in range(8):
                                for (reg, o0, w) in ((0, 0, 512), (1, 512, 512),
                                                     (2, 1024, 1)):
                                    st = first[reg]; first[reg] = False
                                    ev_cross_last = P.emit(
                                        "tensor",
                                        lambda e, j=j, c=c, o0=o0, w=w, st=st, qs=qs:
                                        e.matmul(
                                            pO[:, o0:o0 + w],
                                            qs[:, j * SC + c * 128:j * SC + (c + 1) * 128],
                                            S_b[:, j * 1025 + o0:j * 1025 + o0 + w],
                                            start=st, stop=False),
                                        sem="mm" if (j == 7 and o0 == 1024) else None)
                            ev_S_read = ev_cross_last
                        ev_av_last = None
                        for cp in range(c + 1):
                            P.wait("tensor", evacs[cp])
                            last = (cp == c)
                            for (reg, o0, w) in ((0, 0, 512), (1, 512, 512), (2, 1024, 1)):
                                st = first[reg]; first[reg] = False
                                ev_av_last = P.emit(
                                    "tensor",
                                    lambda e, cp=cp, o0=o0, w=w, st=st, last=last,
                                    vs=vs, pp=pp:
                                    e.matmul(
                                        pO[:, o0:o0 + w],
                                        aT_sb[:, pp + cp * 128:pp + (cp + 1) * 128],
                                        vs[:, cp * 1025 + o0:cp * 1025 + o0 + w],
                                        start=st, stop=last),
                                    sem="mm" if (last and o0 == 1024) else None)
                        for cp in range(c + 1):
                            ev_aT_free[tglob % 2][cp] = ev_av_last
                        if tglob + 1 < NCH:
                            ev_pairs[tglob + 1] = emit_pairs(tglob + 1)
                        cst = c % 4
                        ev_d1 = P.emit("vector", lambda e, cst=cst: e.tensor_scalar(
                            st_den[:, cst:cst + 1], pO[:, 1024:1025], DEN_EPS, None,
                            ALU.add), sem="dve", waits=[ev_av_last],
                            writes=(f"dn{cst}",))
                        ev_d2 = P.emit("vector", lambda e, cst=cst: e.reciprocal(
                            st_rc[:, cst:cst + 1], st_den[:, cst:cst + 1]), sem="dve",
                            reads=(f"dn{cst}",), writes=(f"rc{cst}",))
                        ev_odiv = P.emit("vector", lambda e: e.tensor_copy(
                            odiv_tok[:], pO[:, 0:1024]), sem="dve",
                            waits=[ev_av_last, ev_pT2_free[0], ev_pT2_free[1]])
                        tp = c % 2
                        P.waits("tensor", [ev_odiv, ev_pT2_free[tp]])
                        evt = None
                        for j in range(8):
                            evt = P.emit("tensor", lambda e, j=j, tp=tp: e.transpose(
                                pT2[:, tp * 1024 + j * 128:tp * 1024 + (j + 1) * 128],
                                odiv_tok[:, j * 128:(j + 1) * 128], ident[:]),
                                sem="mm" if j == 7 else None)
                        P.waits("scalar", [evt, ev_odiv_free])
                        ev_fm = P.emit("scalar", lambda e, tp=tp: e.activation(
                            odiv_fm[:], pT2[:, tp * 1024:(tp + 1) * 1024], AF.Identity),
                            sem="act")
                        ev_pT2_free[tp] = ev_fm
                        P.waits("tensor", [ev_fm, ev_wp, ev_odiv])
                        ev_pj = None
                        for j in range(8):
                            P.emit("tensor", lambda e, j=j: e.matmul(
                                pO[:, 0:512], odiv_fm[:, j * 128:(j + 1) * 128],
                                w_proj_b[:, j * 1024:j * 1024 + 512],
                                start=(j == 0), stop=(j == 7)), sem=None)
                            ev_pj = P.emit("tensor", lambda e, j=j: e.matmul(
                                pO[:, 512:1024], odiv_fm[:, j * 128:(j + 1) * 128],
                                w_proj_b[:, j * 1024 + 512:(j + 1) * 1024],
                                start=(j == 0), stop=(j == 7)),
                                sem="mm" if j == 7 else None)
                        ev_odiv_free = ev_pj
                        ev_ao = P.emit("vector", lambda e, cst=cst:
                                       e.scalar_tensor_tensor(
                                           ao_sb[:], pO[:, 0:1024],
                                           st_rc[:, cst:cst + 1], b_proj_t[:],
                                           ALU.mult, ALU.add), sem="dve",
                                       waits=[ev_pj, ev_d2],
                                       reads=(f"rc{cst}",), writes=("ao",))
                        ev_pO_free = ev_ao
                        r3, n3, evn3, _ = ln_stats(4 + cst, ao_sb[:], ev_ao, 1.0, EPS,
                                                   xtok="ao")
                        x1ap = xs[:, c * 1024:(c + 1) * 1024]
                        ev_t3 = P.emit("vector", lambda e, r3=r3, n3=n3: e.tensor_scalar(
                            t3_sb[:], ao_sb[:], r3, n3, ALU.mult, ALU.add), sem="dve",
                            waits=[evn3],
                            reads=("ao", f"r{4 + cst}", f"n{4 + cst}"), writes=("t3",))
                        ev_x1 = P.emit("vector", lambda e, x1ap=x1ap: e.tensor_add(
                            x1ap, x1ap, t3_sb[:]), sem="dve",
                            waits=[ev_ld[par], ev_x1st], reads=("t3",))
                        r4, n4, evn4, ev_r4 = ln_stats(8 + cst, x1ap, ev_x1, 1.0, EPS)
                        ev_xn4 = P.emit("scalar", lambda e, x1ap=x1ap, r4=r4, n4=n4, tp=tp:
                                        e.activation(xn4_tok[tp][:], x1ap, AF.Identity,
                                                     bias=n4, scale=r4), sem="act",
                                        waits=[evn4, ev_xn4st[tp]],
                                        reads=(f"r{8 + cst}", f"n{8 + cst}"))
                        P.wait("sync", ev_x1)
                        ev_x1st = P.emit("sync", lambda e, tglob=tglob, x1ap=x1ap:
                                         e.dma_start(out=x1_sc[tglob * 128:
                                                              (tglob + 1) * 128, :],
                                                     in_=x1ap), sem="sx1", amount=16)
                        P.wait("sync", ev_xn4)
                        ev_xn4st[tp] = P.emit("sync", lambda e, tglob=tglob, tp=tp:
                                              e.dma_start(out=xn4_sc[tglob * 128:
                                                                     (tglob + 1) * 128, :],
                                                          in_=xn4_tok[tp][:]),
                                              sem="sxn0" if tp == 0 else "sxn1",
                                              amount=16)
                    ev_ds_ev = None
                    for j in range(8):
                        if j % 2 == 0:
                            reg = pO
                            P.waits("tensor", [ev_pO_free])
                        else:
                            reg = pA
                            P.waits("tensor", [ev_pA_free[0], ev_pA_free[1]])
                        P.waits("tensor", [ev_ldT[par], ev_ld[par], ev_vones])
                        evm = None
                        for c in range(CPS):
                            for (o0, w) in ((0, 512), (512, 512), (1024, 1)):
                                evm = P.emit("tensor",
                                             lambda e, j=j, c=c, o0=o0, w=w, reg=reg,
                                             kTs=kTs, vs=vs:
                                             e.matmul(reg[:, o0:o0 + w],
                                                      kTs[:, c * 1024 + j * 128:
                                                          c * 1024 + (j + 1) * 128],
                                                      vs[:, c * 1025 + o0:
                                                         c * 1025 + o0 + w],
                                                      start=(c == 0), stop=(c == CPS - 1)),
                                             sem="mm")
                        ev_dve = P.emit("vector", lambda e, j=j, reg=reg: e.tensor_add(
                            S_f[:, j * 1025:(j + 1) * 1025],
                            S_f[:, j * 1025:(j + 1) * 1025],
                            reg[:, 0:1025]), sem="dve",
                            waits=[evm, ev_S_read, ev_Sf0],
                            reads=(f"Sf{j}",), writes=(f"Sf{j}",))
                        if j % 2 == 0:
                            ev_pO_free = ev_dve
                        else:
                            ev_pA_free[0] = ev_dve
                            ev_pA_free[1] = ev_dve
                        ev_ds_ev = P.emit("gpsimd", lambda e, j=j: e.tensor_copy(
                            S_b[:, j * 1025:(j + 1) * 1025],
                            S_f[:, j * 1025:(j + 1) * 1025]), sem="pool",
                            reads=(f"Sf{j}",))
                    ev_S_cast = ev_ds_ev
                    ev_buf_free[par] = [evm, ev_x1st]
                p2_done = P.snapshot()
            P.run_block()

        if 4 in phases:
            with ExitStack() as p4:
                def sb4(name, shape, dtype):
                    return p4.enter_context(nc.sbuf_tensor(name, shape, dtype))
                wg_b = sb4("wg_b", [128, 8 * 3072], BF16)
                wu_b = sb4("wu_b", [128, 8 * 3072], BF16)
                xn4_fm = [sb4(f"xn4fm{i}", [128, 8 * 512], BF16) for i in range(2)]
                h_sb = [sb4(f"h_sb{i}", [128, 24 * 512], BF16) for i in range(2)]
                silu_s = [sb4(f"silu{i}", [128, 512], F32) for i in range(2)]
                pg = [p4.enter_context(nc.psum_tensor(f"pg{i}", [128, 512], F32))
                      for i in range(2)]
                pu = [p4.enter_context(nc.psum_tensor(f"pu{i}", [128, 512], F32))
                      for i in range(2)]

                if 2 in phases:
                    P.barrier(p2_done)
                P.wait("sync", ev_cast_wg)
                P.wait("sync", ev_cast_wu)
                ev_wg4 = None
                for kk in range(8):
                    ev_wg4 = P.emit("sync", lambda e, kk=kk: e.dma_start(
                        out=wg_b[:, kk * 3072:(kk + 1) * 3072],
                        in_=wg_bf[kk * 128:(kk + 1) * 128, :]), sem="lwg", amount=16)
                P.wait("gpsimd", ev_cast_wu)
                ev_w4 = None
                for kk in range(8):
                    ev_w4 = P.emit("gpsimd", lambda e, kk=kk: e.dma_start(
                        out=wu_b[:, kk * 3072:(kk + 1) * 3072],
                        in_=wu_bf[kk * 128:(kk + 1) * 128, :]), sem="lwu", amount=16)

                ev_xnl = [None, None]
                ev_xn_free = [None, None]
                ev_pg_free = [None, None]
                ev_pu_free = [None, None]
                ev_silu_free = [None, None]
                ev_hst = [None, None]

                def load_xn4(t):
                    par = t % 2
                    P.wait("scalar", ev_xn_free[par])
                    ev = None
                    for j in range(8):
                        ev = P.emit("scalar", lambda e, t=t, j=j, par=par: e.dma_start(
                            out=xn4_fm[par][:, j * 512:(j + 1) * 512],
                            in_=xn4_sc[t * 512:(t + 1) * 512, j * 128:(j + 1) * 128],
                            transpose=True), sem="dta" if par == 0 else "dtb",
                            amount=16)
                    ev_xnl[par] = ev

                load_xn4(0)
                for t in range(NT5):
                    par = t % 2
                    if t + 1 < NT5:
                        load_xn4(t + 1)
                    ev_mm_xn = None
                    ev_h_last = None
                    for hc in range(24):
                        sl = hc % 2
                        P.waits("tensor", [ev_xnl[par], ev_wg4, ev_pg_free[sl],
                                           ev_pu_free[sl]])
                        evg = evu = None
                        for kk in range(8):
                            evg = P.emit("tensor", lambda e, hc=hc, kk=kk, sl=sl, par=par:
                                         e.matmul(pg[sl][:],
                                                  wg_b[:, kk * 3072 + hc * 128:
                                                       kk * 3072 + (hc + 1) * 128],
                                                  xn4_fm[par][:, kk * 512:(kk + 1) * 512],
                                                  start=(kk == 0), stop=(kk == 7)),
                                         sem="mm")
                        P.wait("tensor", ev_w4)
                        for kk in range(8):
                            evu = P.emit("tensor", lambda e, hc=hc, kk=kk, sl=sl, par=par:
                                         e.matmul(pu[sl][:],
                                                  wu_b[:, kk * 3072 + hc * 128:
                                                       kk * 3072 + (hc + 1) * 128],
                                                  xn4_fm[par][:, kk * 512:(kk + 1) * 512],
                                                  start=(kk == 0), stop=(kk == 7)),
                                         sem="mm")
                        ev_mm_xn = evu
                        P.waits("scalar", [evg, ev_silu_free[sl]])
                        ev_sl = P.emit("scalar", lambda e, sl=sl: e.activation(
                            silu_s[sl][:], pg[sl][:], AF.Silu, scale=QSCALE), sem="act")
                        ev_pg_free[sl] = ev_sl
                        P.waits("vector", [ev_sl, evu, ev_hst[par]])
                        ev_h = P.emit("vector", lambda e, hc=hc, sl=sl, par=par:
                                      e.scalar_tensor_tensor(
                                          h_sb[par][:, hc * 512:(hc + 1) * 512],
                                          pu[sl][:], QSCALE, silu_s[sl][:],
                                          ALU.mult, ALU.mult), sem="dve")
                        ev_pu_free[sl] = ev_h
                        ev_silu_free[sl] = ev_h
                        ev_h_last = ev_h
                    ev_xn_free[par] = ev_mm_xn
                    P.wait("sync", ev_h_last)
                    ev_hst[par] = P.emit("sync", lambda e, t=t, par=par: e.dma_start(
                        out=h_sc[t], in_=h_sb[par][:]),
                        sem="sh0" if par == 0 else "sh1", amount=16)
                p4_done = P.snapshot()
            P.run_block()

        if 5 in phases:
            with ExitStack() as p5:
                def sb5(name, shape, dtype):
                    return p5.enter_context(nc.sbuf_tensor(name, shape, dtype))
                wd_b = sb5("wd_b", [128, 24 * 1024], BF16)
                h5 = [sb5(f"h5_{i}", [128, 24 * 512], BF16) for i in range(2)]
                x15 = [sb5(f"x15_{i}", [128, 4 * 1024], F32) for i in range(2)]
                y_sb = sb5("y_sb", [128, 1024], F32)
                t5_sb = sb5("t5_sb", [128, 1024], F32)
                py = [p5.enter_context(nc.psum_tensor(f"py{i}", [128, 1024], F32))
                      for i in range(2)]

                if 4 in phases:
                    P.barrier(p4_done)
                ev_w5 = None
                P.wait("sync", ev_cast_wd)
                P.wait("gpsimd", ev_cast_wd)
                for kk in range(24):
                    eng5 = "sync" if kk % 2 == 0 else "gpsimd"
                    sem5 = "lwd" if kk % 2 == 0 else "dwp"
                    ev_w5a = P.emit(eng5, lambda e, kk=kk: e.dma_start(
                        out=wd_b[:, kk * 1024:(kk + 1) * 1024],
                        in_=wd_bf[kk * 128:(kk + 1) * 128, :]), sem=sem5, amount=16)
                    if kk % 2 == 0:
                        ev_w5 = ev_w5a
                    else:
                        ev_w5b = ev_w5a

                ev_l5 = [None, None]
                ev_b5_free = [None, None]
                ev_py_free = [None, None]
                ev_y_free = None
                ev_ost = None

                def load5(t):
                    par = t % 2
                    if ev_b5_free[par] is not None:
                        P.waits("sync", ev_b5_free[par])
                    dsem = "da" if par == 0 else "db"
                    P.emit("sync", lambda e, t=t, par=par: e.dma_start(
                        out=h5[par][:], in_=h_sc[t]), sem=dsem, amount=16)
                    evs = None
                    for c in range(4):
                        evs = P.emit("sync", lambda e, t=t, c=c, par=par: e.dma_start(
                            out=x15[par][:, c * 1024:(c + 1) * 1024],
                            in_=x1_sc[t * 512 + c * 128:t * 512 + (c + 1) * 128, :]),
                            sem=dsem, amount=16)
                    ev_l5[par] = evs

                load5(0)
                for t in range(NT5):
                    par = t % 2
                    if t + 1 < NT5:
                        load5(t + 1)
                    ev_mm5 = None
                    for c in range(4):
                        sl = c % 2
                        P.waits("tensor", [ev_l5[par], ev_w5, ev_w5b, ev_py_free[sl]])
                        evm = None
                        for hc in range(24):
                            P.emit("tensor", lambda e, hc=hc, c=c, sl=sl, par=par:
                                   e.matmul(py[sl][:, 0:512],
                                            h5[par][:, hc * 512 + c * 128:
                                                    hc * 512 + (c + 1) * 128],
                                            wd_b[:, hc * 1024:hc * 1024 + 512],
                                            start=(hc == 0), stop=(hc == 23)), sem="mm")
                            evm = P.emit("tensor", lambda e, hc=hc, c=c, sl=sl, par=par:
                                         e.matmul(py[sl][:, 512:1024],
                                                  h5[par][:, hc * 512 + c * 128:
                                                          hc * 512 + (c + 1) * 128],
                                                  wd_b[:, hc * 1024 + 512:
                                                       (hc + 1) * 1024],
                                                  start=(hc == 0), stop=(hc == 23)),
                                         sem="mm")
                        ev_mm5 = evm
                        cst = c % 4
                        ev_y = P.emit("scalar", lambda e, sl=sl: e.activation(
                            y_sb[:], py[sl][:], AF.Identity, scale=QSCALE), sem="act",
                            waits=[evm, ev_y_free], writes=("y5",))
                        ev_py_free[sl] = ev_y
                        r5, n5, evn5, _ = ln_stats(12 + cst, y_sb[:], ev_y, 1.0, EPS,
                                                   xtok="y5")
                        ev_t5 = P.emit("vector", lambda e, r5=r5, n5=n5: e.tensor_scalar(
                            t5_sb[:], y_sb[:], r5, n5, ALU.mult, ALU.add), sem="dve",
                            waits=[evn5],
                            reads=("y5", f"r{12 + cst}", f"n{12 + cst}"), writes=("t5",))
                        ev_y_free = ev_t5
                        x1ap = x15[par][:, c * 1024:(c + 1) * 1024]
                        ev_o = P.emit("vector", lambda e, x1ap=x1ap: e.tensor_add(
                            x1ap, x1ap, t5_sb[:]), sem="dve", reads=("t5",))
                        P.wait("sync", ev_o)
                        ev_ost = P.emit("sync", lambda e, t=t, c=c, x1ap=x1ap:
                                        e.dma_start(out=out_d[t * 512 + c * 128:
                                                              t * 512 + (c + 1) * 128, :],
                                                    in_=x1ap), sem="so", amount=16)
                    ev_b5_free[par] = [ev_mm5, ev_ost]
                P.wait("sync", ev_ost)
            P.run_block()
        else:
            pass

    return nc


_CACHE = {}


def kernel(L0_attn_x, L0_attn_w_qkv, L0_attn_b_qkv, L0_attn_w_gate, L0_attn_b_gate,
           L0_attn_w_proj, L0_attn_b_proj, L0_attn_ln_g, L0_attn_ln_b,
           L0_ffn_w_gate, L0_ffn_w_up, L0_ffn_w_down):
    x = np.asarray(L0_attn_x, dtype=np.float32)
    B, T, _ = x.shape
    w_qkv = np.ascontiguousarray(np.asarray(L0_attn_w_qkv, dtype=np.float32))
    w_gate = np.ascontiguousarray(np.asarray(L0_attn_w_gate, dtype=np.float32))
    w_proj = np.ascontiguousarray(np.asarray(L0_attn_w_proj, dtype=np.float32))
    b_qkv = np.asarray(L0_attn_b_qkv, dtype=np.float32)
    b_gate = np.asarray(L0_attn_b_gate, dtype=np.float32)
    b_proj = np.asarray(L0_attn_b_proj, dtype=np.float32)
    ln_g = np.asarray(L0_attn_ln_g, dtype=np.float32)
    ln_b = np.asarray(L0_attn_ln_b, dtype=np.float32)
    wg_t = np.ascontiguousarray(np.asarray(L0_ffn_w_gate, dtype=np.int32).T)
    wu_t = np.ascontiguousarray(np.asarray(L0_ffn_w_up, dtype=np.int32).T)
    wd_t = np.ascontiguousarray(np.asarray(L0_ffn_w_down, dtype=np.int32).T)

    b_qkv_fm = np.ascontiguousarray(b_qkv.reshape(24, 128).T)
    b_gate_fm = np.ascontiguousarray(b_gate.reshape(8, 128).T)
    ln_g_fm = np.ascontiguousarray(ln_g.reshape(8, 128).T)
    ln_b_fm = np.ascontiguousarray(ln_b.reshape(8, 128).T)
    b_proj_t = np.ascontiguousarray(np.broadcast_to(b_proj, (128, D)))
    b_v_t = np.ascontiguousarray(np.broadcast_to(b_qkv[2 * D:3 * D], (128, D)))

    if T not in _CACHE:
        _CACHE[T] = build_nc(T)
    nc = _CACHE[T]
    shared = dict(w_qkv=w_qkv, w_gate=w_gate, w_proj=w_proj, b_qkv_fm=b_qkv_fm,
                  b_gate_fm=b_gate_fm, b_proj_t=b_proj_t, b_v_t=b_v_t,
                  ln_g_fm=ln_g_fm, ln_b_fm=ln_b_fm, wg_t=wg_t, wu_t=wu_t, wd_t=wd_t)
    in_maps = [dict(shared, x=np.ascontiguousarray(x[i])) for i in range(B)]
    res = run_bass_kernel_spmd(nc, in_maps, core_ids=list(range(B)))
    return np.stack([np.asarray(res.results[i]["out"]) for i in range(B)])


# version 16
# speedup vs baseline: 1.0015x; 1.0015x over previous
import numpy as np
from contextlib import ExitStack

import concourse.bass as bass
import concourse.mybir as mybir
from concourse.bass_utils import run_bass_kernel_spmd

F32 = mybir.dt.float32
BF16 = mybir.dt.bfloat16
I32 = mybir.dt.int32
AF = mybir.ActivationFunctionType
ALU = mybir.AluOpType
AX = mybir.AxisListType

D = 1024
HID = 3072
NCORE = 8
EPS = 1e-5
DEN_EPS = 1e-6
QSCALE = 1.0 / 127.0


class Prog:

    ENGINES = ("sync", "gpsimd", "scalar", "vector", "tensor")

    def __init__(self, nc, ctx):
        self.nc = nc
        self.cnt = {}
        self.h = {}
        for name in ("mm", "dve", "act", "pool", "da", "db", "dta", "dtb", "dw",
                     "dwp", "cwp", "cwg", "cwu", "cwd", "lwg", "lwu", "lwd",
                     "sq", "sk", "skp", "sv", "sx1", "sxn0", "sxn1",
                     "sh0", "sh1", "so"):
            self.h[name] = ctx.enter_context(nc.semaphore(name))
            self.cnt[name] = 0
        self.track = {}
        self._reset()

    def _reset(self):
        self.ops = {e: [] for e in self.ENGINES}
        self.lastwait = {}

    def inc(self, name, amount=1):
        self.cnt[name] += amount
        return (name, self.cnt[name])

    def wait(self, eng, ev):
        if ev is None:
            return
        name, val = ev
        if val <= 0:
            return
        key = (eng, name)
        if self.lastwait.get(key, -1) >= val:
            return
        self.lastwait[key] = val
        h = self.h[name]
        self.ops[eng].append(lambda e, h=h, v=val: e.wait_ge(h, v))

    def waits(self, eng, evs):
        for ev in evs:
            self.wait(eng, ev)

    def emit(self, eng, fn, sem=None, amount=1, reads=(), writes=(), waits=()):
        for ev in waits:
            self.wait(eng, ev)
        for t in reads:
            rec = self.track.get(t)
            if rec is not None and rec[0] is not None:
                self.wait(eng, rec[0][0])
        for t in writes:
            rec = self.track.get(t)
            if rec is not None:
                w = rec[0]
                if w is not None and not (eng == "tensor" and w[1] == "tensor"
                                          and t.startswith("ps:")):
                    self.wait(eng, w[0])
                for rev, _ in rec[1]:
                    self.wait(eng, rev)
        if sem is not None:
            ev = self.inc(sem, amount)
            h = self.h[sem]
            self.ops[eng].append(lambda e, fn=fn, h=h, a=amount: fn(e).then_inc(h, a))
        else:
            assert not reads and not writes, "tracked ops need a sem"
            self.ops[eng].append(fn)
            return None
        for t in reads:
            self.track.setdefault(t, [None, []])[1].append((ev, eng))
        for t in writes:
            self.track[t] = [(ev, eng), []]
        return ev

    def snapshot(self):
        return dict(self.cnt)

    def barrier(self, snap):
        for eng in self.ENGINES:
            for name, val in snap.items():
                self.wait(eng, (name, val))

    def run_block(self):
        with self.nc.Block() as block:
            for eng in self.ENGINES:
                ops = self.ops[eng]
                getattr(block, eng)(lambda e, ops=ops: [op(e) for op in ops])
        self._reset()


def build_nc(T=4096, phases=(1, 2, 4, 5), dev_scratch=False):
    nc = bass.Bass("TRN2", target_bir_lowering=False)
    NCH = T // 128
    SC = 512
    CPS = SC // 128
    NSC = T // SC
    NT5 = T // 512

    x_d = nc.dram_tensor("x", [T, D], F32, kind="ExternalInput").ap()
    w_qkv_d = nc.dram_tensor("w_qkv", [D, 3 * D], F32, kind="ExternalInput").ap()
    w_gate_d = nc.dram_tensor("w_gate", [D, D], F32, kind="ExternalInput").ap()
    w_proj_d = nc.dram_tensor("w_proj", [D, D], F32, kind="ExternalInput").ap()
    b_qkv_d = nc.dram_tensor("b_qkv_fm", [128, 24], F32, kind="ExternalInput").ap()
    b_gate_d = nc.dram_tensor("b_gate_fm", [128, 8], F32, kind="ExternalInput").ap()
    b_proj_d = nc.dram_tensor("b_proj_t", [128, D], F32, kind="ExternalInput").ap()
    b_v_d = nc.dram_tensor("b_v_t", [128, D], F32, kind="ExternalInput").ap()
    ln_g_d = nc.dram_tensor("ln_g_fm", [128, 8], F32, kind="ExternalInput").ap()
    ln_b_d = nc.dram_tensor("ln_b_fm", [128, 8], F32, kind="ExternalInput").ap()
    wg_d = nc.dram_tensor("wg_t", [D, HID], I32, kind="ExternalInput").ap()
    wu_d = nc.dram_tensor("wu_t", [D, HID], I32, kind="ExternalInput").ap()
    wd_d = nc.dram_tensor("wd_t", [HID, D], I32, kind="ExternalInput").ap()
    out_d = nc.dram_tensor("out", [T, D], F32, kind="ExternalOutput").ap()

    skind = "ExternalOutput" if dev_scratch else "Internal"
    if dev_scratch:
        dbg_xn = nc.dram_tensor("dbg_xn", [NSC, 128, 8 * SC], BF16, kind=skind).ap()
        dbg_gate = nc.dram_tensor("dbg_gate", [NSC, 128, SC], F32, kind=skind).ap()
        dbg_xntok = nc.dram_tensor("dbg_xntok", [NSC, 128, 4 * 1024], BF16, kind=skind).ap()
        dbg_ident = nc.dram_tensor("dbg_ident", [128, 128], BF16, kind=skind).ap()
        dbg_stats = nc.dram_tensor("dbg_stats", [NSC, 128, 128], F32, kind=skind).ap()
    q_sc = nc.dram_tensor("q_sc", [NSC, 128, 8 * SC], BF16, kind=skind).ap()
    k_sc = nc.dram_tensor("k_sc", [NSC, 128, 8 * SC], BF16, kind=skind).ap()
    k_pl = nc.dram_tensor("k_pl", [D, T], BF16, kind=skind).ap()
    v_pl = nc.dram_tensor("v_pl", [T, D], BF16, kind=skind).ap()
    x1_sc = nc.dram_tensor("x1_sc", [T, D], F32, kind=skind).ap()
    xn4_sc = nc.dram_tensor("xn4_sc", [T, D], BF16, kind=skind).ap()
    h_sc = nc.dram_tensor("h_sc", [NT5, 128, 24 * 512], BF16, kind=skind).ap()
    wp_bf = nc.dram_tensor("wp_bf", [D, D], BF16, kind="Internal").ap()
    wg_bf = nc.dram_tensor("wg_bf", [D, HID], BF16, kind="Internal").ap()
    wu_bf = nc.dram_tensor("wu_bf", [D, HID], BF16, kind="Internal").ap()
    wd_bf = nc.dram_tensor("wd_bf", [HID, D], BF16, kind="Internal").ap()

    ctx = ExitStack()
    with ctx:
        P = Prog(nc, ctx)

        def sb(name, shape, dtype):
            return ctx.enter_context(nc.sbuf_tensor(name, shape, dtype))

        def ps(name, shape, dtype=F32):
            return ctx.enter_context(nc.psum_tensor(name, shape, dtype))

        ident = sb("ident", [128, 128], BF16)
        mask_keep = sb("mask_keep", [128, 128], BF16)
        b_qkv = sb("c_b_qkv", [128, 24], F32)
        b_gate = sb("c_b_gate", [128, 8], F32)
        nb_gate = sb("nb_gate", [128, 8], F32)
        b_proj_t = sb("c_b_proj", [128, D], F32)
        b_v_t = sb("c_b_v", [128, D], F32)
        ln_g = sb("c_ln_g", [128, 8], F32)
        ln_b = sb("c_ln_b", [128, 8], F32)
        sq_dump = sb("sq_dump", [128, 1024], F32)
        stats = sb("stats", [128, 128], F32)

        ev_consts = None
        for dst, src in ((b_qkv, b_qkv_d), (b_gate, b_gate_d), (b_proj_t, b_proj_d),
                         (b_v_t, b_v_d), (ln_g, ln_g_d), (ln_b, ln_b_d)):
            ev_consts = P.emit("sync", lambda e, d=dst, s=src: e.dma_start(out=d[:], in_=s),
                               sem="dw", amount=16)
        P.wait("vector", ev_consts)
        P.wait("scalar", ev_consts)
        ev_nbg = P.emit("vector", lambda e: e.tensor_scalar(
            nb_gate[:], b_gate[:], -1.0, None, ALU.mult), sem="dve")
        ev0 = P.emit("gpsimd", lambda e: e.memset(ident[:], 0.0), sem="pool")
        P.wait("gpsimd", ev0)
        ev_ident = P.emit("gpsimd", lambda e: e.affine_select(
            out=ident[:], in_=ident[:], compare_op=ALU.not_equal, fill=1.0,
            base=0, pattern=[[-1, 128]], channel_multiplier=1), sem="pool")
        ev0 = P.emit("gpsimd", lambda e: e.memset(mask_keep[:], 1.0), sem="pool")
        P.wait("gpsimd", ev0)
        ev_mask = P.emit("gpsimd", lambda e: e.affine_select(
            out=mask_keep[:], in_=mask_keep[:], compare_op=ALU.is_ge, fill=0.0,
            base=0, pattern=[[1, 128]], channel_multiplier=-1), sem="pool")

        st_ss = stats[:, 0:16]
        st_sum = stats[:, 16:32]
        st_m = stats[:, 32:48]
        st_v = stats[:, 48:64]
        st_r = stats[:, 64:80]
        st_nmr = stats[:, 80:96]
        st_tmp = stats[:, 96:112]
        st_den = stats[:, 112:116]
        st_rc = stats[:, 116:120]

        def ln_stats(col, x_ap, ev_x, scale_var, bias_var, xtok=None):
            c = col
            xr = (xtok,) if xtok else ()
            ev_s = P.emit("vector", lambda e: e.reduce_sum(
                st_sum[:, c:c + 1], x_ap, axis=AX.X), sem="dve",
                waits=[ev_x], reads=xr, writes=(f"su{c}",))
            ev_q = P.emit("scalar", lambda e: e.activation(
                sq_dump[:], x_ap, AF.Square, accum_out=st_ss[:, c:c + 1]), sem="act",
                waits=[ev_x], reads=xr, writes=("sqd", f"ss{c}"))
            ev_m = P.emit("vector", lambda e: e.tensor_scalar(
                st_m[:, c:c + 1], st_sum[:, c:c + 1], 1.0 / D, None, ALU.mult),
                sem="dve", reads=(f"su{c}",), writes=(f"m{c}",))
            P.emit("vector", lambda e: e.scalar_tensor_tensor(
                st_tmp[:, c:c + 1], st_m[:, c:c + 1], scale_var, st_m[:, c:c + 1],
                ALU.mult, ALU.mult), sem="dve", reads=(f"m{c}",), writes=(f"tp{c}",))
            P.emit("vector", lambda e: e.tensor_scalar(
                st_v[:, c:c + 1], st_ss[:, c:c + 1], scale_var / D, bias_var,
                ALU.mult, ALU.add), sem="dve", reads=(f"ss{c}",), writes=(f"v{c}",))
            ev_v = P.emit("vector", lambda e: e.tensor_sub(
                st_v[:, c:c + 1], st_v[:, c:c + 1], st_tmp[:, c:c + 1]), sem="dve",
                reads=(f"v{c}", f"tp{c}"), writes=(f"v{c}",))
            P.emit("scalar", lambda e: e.activation(
                st_tmp[:, c:c + 1], st_v[:, c:c + 1], AF.Ln), sem="act",
                reads=(f"v{c}",), writes=(f"tp{c}",))
            ev_r = P.emit("scalar", lambda e: e.activation(
                st_r[:, c:c + 1], st_tmp[:, c:c + 1], AF.Exp, scale=-0.5), sem="act",
                reads=(f"tp{c}",), writes=(f"r{c}",))
            ev_n = P.emit("vector", lambda e: e.scalar_tensor_tensor(
                st_nmr[:, c:c + 1], st_m[:, c:c + 1], -1.0, st_r[:, c:c + 1],
                ALU.mult, ALU.mult), sem="dve",
                reads=(f"m{c}", f"r{c}"), writes=(f"n{c}",))
            return st_r[:, c:c + 1], st_nmr[:, c:c + 1], ev_n, ev_r

        if 1 in phases:
            with ExitStack() as p1:
                def sb1(name, shape, dtype):
                    return p1.enter_context(nc.sbuf_tensor(name, shape, dtype))
                w_qkv_b = sb1("w_qkv_b", [128, 8 * 3072], BF16)
                w_gate_b = sb1("w_gate_b", [128, 8 * 1024], BF16)
                xb = [sb1("xb0", [128, CPS * 1024], F32),
                      sb1("xb1", [128, CPS * 1024], F32)]
                xn_tok = sb1("xn_tok", [128, CPS * 1024], BF16)
                xn_fm = sb1("xn_fm", [128, 8 * SC], BF16)
                q_fm = sb1("q_fm", [128, 8 * SC], BF16)
                k_fm = sb1("k_fm", [128, 8 * SC], BF16)
                v_tk = sb1("v_tk", [128, CPS * 1024], BF16)
                scr = [sb1(f"scr{i}", [128, SC], F32) for i in range(10)]
                pm = [p1.enter_context(nc.psum_tensor(f"pm{i}", [128, 512], F32))
                      for i in range(4)]
                pT = p1.enter_context(nc.psum_tensor("pT", [128, 2048], BF16))

                ev_w1 = None
                for kk in range(8):
                    ev_w1 = P.emit("gpsimd", lambda e, kk=kk: e.dma_start(
                        out=w_qkv_b[:, kk * 3072:(kk + 1) * 3072],
                        in_=w_qkv_d[kk * 128:(kk + 1) * 128, :]), sem="dwp", amount=16)
                for kk in range(8):
                    ev_w1 = P.emit("gpsimd", lambda e, kk=kk: e.dma_start(
                        out=w_gate_b[:, kk * 1024:(kk + 1) * 1024],
                        in_=w_gate_d[kk * 128:(kk + 1) * 128, :]), sem="dwp", amount=16)
                ev_cast_wp = P.emit("gpsimd", lambda e: e.dma_start(
                    out=wp_bf, in_=w_proj_d), sem="cwp", amount=16)
                ev_cast_wg = P.emit("gpsimd", lambda e: e.dma_start(
                    out=wg_bf, in_=wg_d), sem="cwg", amount=16)
                ev_cast_wu = P.emit("gpsimd", lambda e: e.dma_start(
                    out=wu_bf, in_=wu_d), sem="cwu", amount=16)
                ev_cast_wd = P.emit("gpsimd", lambda e: e.dma_start(
                    out=wd_bf, in_=wd_d), sem="cwd", amount=16)

                ev_x = [None, None]
                ev_xb_free = [None, None]
                ev_pm_free = [None, None, None, None]
                ev_pT_free = [None, None]
                ev_mm_last = None
                ev_store = {"q": None, "k": None, "kp": None, "v": None}

                def load_x(g):
                    par = g % 2
                    evs = []
                    if ev_xb_free[par] is not None:
                        P.waits("sync", ev_xb_free[par])
                    for c in range(CPS):
                        evs.append(P.emit("sync", lambda e, g=g, c=c, par=par:
                                          e.dma_start(out=xb[par][:, c * 1024:(c + 1) * 1024],
                                                      in_=x_d[g * SC + c * 128:
                                                              g * SC + (c + 1) * 128, :]),
                                          sem="da" if par == 0 else "db", amount=16))
                    ev_x[par] = evs[-1]

                def emit_ln12(g):
                    par = g % 2
                    xcur = xb[par]
                    sv = 1.0 + EPS
                    bv = EPS * EPS
                    ev_x_g = ev_x[par]
                    xc = [xcur[:, c * 1024:(c + 1) * 1024] for c in range(CPS)]
                    for c in range(CPS):
                        P.emit("vector", lambda e, c=c: e.reduce_sum(
                            st_sum[:, c:c + 1], xc[c], axis=AX.X), sem="dve",
                            waits=[ev_x_g], writes=(f"su{c}",))
                    for c in range(CPS):
                        P.emit("scalar", lambda e, c=c: e.activation(
                            sq_dump[:], xc[c], AF.Square,
                            accum_out=st_ss[:, c:c + 1]), sem="act",
                            waits=[ev_x_g], writes=("sqd", f"ss{c}"))
                    for c in range(CPS):
                        P.emit("vector", lambda e, c=c: e.tensor_scalar(
                            st_m[:, c:c + 1], st_sum[:, c:c + 1], 1.0 / D, None,
                            ALU.mult), sem="dve", reads=(f"su{c}",),
                            writes=(f"m{c}",))
                    for c in range(CPS):
                        P.emit("vector", lambda e, c=c: e.scalar_tensor_tensor(
                            st_tmp[:, c:c + 1], st_m[:, c:c + 1], sv,
                            st_m[:, c:c + 1], ALU.mult, ALU.mult), sem="dve",
                            reads=(f"m{c}",), writes=(f"tp{c}",))
                    for c in range(CPS):
                        P.emit("vector", lambda e, c=c: e.tensor_scalar(
                            st_v[:, c:c + 1], st_ss[:, c:c + 1], sv / D, bv,
                            ALU.mult, ALU.add), sem="dve", reads=(f"ss{c}",),
                            writes=(f"v{c}",))
                    for c in range(CPS):
                        P.emit("vector", lambda e, c=c: e.tensor_sub(
                            st_v[:, c:c + 1], st_v[:, c:c + 1],
                            st_tmp[:, c:c + 1]), sem="dve",
                            reads=(f"v{c}", f"tp{c}"), writes=(f"v{c}",))
                    for c in range(CPS):
                        P.emit("scalar", lambda e, c=c: e.activation(
                            st_tmp[:, c:c + 1], st_v[:, c:c + 1], AF.Ln),
                            sem="act", reads=(f"v{c}",), writes=(f"tp{c}",))
                    for c in range(CPS):
                        P.emit("scalar", lambda e, c=c: e.activation(
                            st_r[:, c:c + 1], st_tmp[:, c:c + 1], AF.Exp,
                            scale=-0.5), sem="act", reads=(f"tp{c}",),
                            writes=(f"r{c}",))
                    ev_n = None
                    for c in range(CPS):
                        ev_n = P.emit("vector", lambda e, c=c: e.scalar_tensor_tensor(
                            st_nmr[:, c:c + 1], st_m[:, c:c + 1], -1.0,
                            st_r[:, c:c + 1], ALU.mult, ALU.mult), sem="dve",
                            reads=(f"m{c}", f"r{c}"), writes=(f"n{c}",))
                    evn = []
                    for c in range(CPS):
                        ev = P.emit("vector", lambda e, c=c: e.tensor_scalar(
                            xn_tok[:, c * 1024:(c + 1) * 1024],
                            xc[c], st_r[:, c:c + 1], st_nmr[:, c:c + 1],
                            ALU.mult, ALU.add), sem="dve",
                            waits=[ev_pT_free[0], ev_pT_free[1]],
                            reads=(f"r{c}", f"n{c}"))
                        evn.append(ev)
                    ev_xb_free[par] = [evn[-1], ev_n]
                    return evn

                load_x(0)
                evn = emit_ln12(0)
                for g in range(NSC):
                    par = g % 2
                    if g + 1 < NSC:
                        load_x(g + 1)
                    ev_xn = []
                    for j in range(8):
                        tpar = j % 2
                        P.wait("tensor", evn[-1])
                        P.wait("tensor", ev_ident)
                        P.wait("tensor", ev_pT_free[tpar])
                        P.wait("tensor", ev_mm_last)
                        evt = None
                        for c in range(CPS):
                            evt = P.emit("tensor", lambda e, c=c, j=j, tpar=tpar:
                                         e.transpose(pT[:, tpar * 1024 + c * 128:
                                                        tpar * 1024 + (c + 1) * 128],
                                                     xn_tok[:, c * 1024 + j * 128:
                                                            c * 1024 + (j + 1) * 128],
                                                     ident[:]), sem="mm")
                        P.wait("scalar", evt)
                        ev = P.emit("scalar", lambda e, j=j, tpar=tpar:
                                    e.activation(xn_fm[:, j * SC:(j + 1) * SC],
                                                 pT[:, tpar * 1024:tpar * 1024 + 512],
                                                 AF.Identity, bias=ln_b[:, j:j + 1],
                                                 scale=ln_g[:, j:j + 1]), sem="act")
                        ev_pT_free[tpar] = ev
                        ev_xn.append(ev)
                    slot_rr = [0]

                    def mm8(wsb, col0, extra_waits=()):
                        psl = slot_rr[0]
                        slot_rr[0] = (psl + 1) % 4
                        P.wait("tensor", ev_xn[-1])
                        P.wait("tensor", ev_w1)
                        P.wait("tensor", ev_pm_free[psl])
                        for w in extra_waits:
                            P.wait("tensor", w)
                        ev = None
                        for kk in range(8):
                            ev = P.emit("tensor", lambda e, psl=psl, wsb=wsb, kk=kk, col0=col0:
                                        e.matmul(pm[psl][:],
                                                 wsb[:, kk * (wsb.shape[1] // 8) + col0:
                                                     kk * (wsb.shape[1] // 8) + col0 + 128],
                                                 xn_fm[:, kk * SC:(kk + 1) * SC],
                                                 start=(kk == 0), stop=(kk == 7)),
                                        sem="mm")
                        return psl, ev

                    pend_q = []

                    def emit_gate_k(j):
                        sA, sB, sC, sD, sE = (scr[(j % 2) * 5 + i] for i in range(5))
                        tA, tB, tC, tD, tE = (f"scr{(j % 2) * 5 + i}" for i in range(5))
                        gsl, evg = mm8(w_gate_b, j * 128)
                        ev_e1 = P.emit("scalar", lambda e, sA=sA, j=j, gsl=gsl:
                                       e.activation(sA[:], pm[gsl][:], AF.Exp,
                                                    scale=-1.0,
                                                    bias=nb_gate[:, j:j + 1]),
                                       sem="act", waits=[evg, ev_nbg], writes=(tA,))
                        ev_pm_free[gsl] = ev_e1
                        P.emit("scalar", lambda e, sA=sA, sB=sB:
                               e.activation(sB[:], sA[:], AF.Ln, bias=1.0),
                               sem="act", reads=(tA,), writes=(tB,))
                        ev_gate = P.emit("scalar", lambda e, sB=sB, sA=sA: e.activation(
                            sA[:], sB[:], AF.Exp, scale=-1.0), sem="act",
                            reads=(tB,), writes=(tA,))
                        ksl, evk = mm8(w_qkv_b, 1024 + j * 128)
                        ev_stt = P.emit("vector", lambda e, sC=sC, j=j, sA=sA, ksl=ksl:
                                        e.scalar_tensor_tensor(
                                            sC[:], pm[ksl][:], b_qkv[:, 8 + j:9 + j],
                                            sA[:], ALU.add, ALU.mult), sem="dve",
                                        waits=[evk], reads=(tA,), writes=(tC,))
                        ev_pm_free[ksl] = ev_stt
                        P.emit("vector", lambda e, sC=sC, sD=sD:
                               e.tensor_scalar_max(sD[:], sC[:], 0.0), sem="dve",
                               reads=(tC,), writes=(tD,))
                        P.emit("vector", lambda e, sC=sC: e.tensor_scalar_min(
                            sC[:], sC[:], 0.0), sem="dve", reads=(tC,), writes=(tC,))
                        P.emit("scalar", lambda e, sC=sC, sE=sE: e.activation(
                            sE[:], sC[:], AF.Exp), sem="act",
                            reads=(tC,), writes=(tE,))
                        ev_kj = P.emit("vector", lambda e, sD=sD, sE=sE, j=j:
                                       e.tensor_add(k_fm[:, j * SC:(j + 1) * SC],
                                                    sD[:], sE[:]), sem="dve",
                                       waits=[ev_store["k"], ev_store["kp"]],
                                       reads=(tD, tE))
                        return ev_kj

                    def emit_q(j):
                        sA, sB, sC, sD, sE = (scr[(j % 2) * 5 + i] for i in range(5))
                        tA, tB, tC, tD, tE = (f"scr{(j % 2) * 5 + i}" for i in range(5))
                        qsl, evq = mm8(w_qkv_b, j * 128)
                        P.emit("vector", lambda e, sD=sD, j=j, qsl=qsl: e.tensor_scalar(
                            sD[:], pm[qsl][:], b_qkv[:, j:j + 1], 0.0, ALU.add,
                            ALU.max), sem="dve", waits=[evq], writes=(tD,))
                        ev_pm_free[qsl] = P.emit(
                            "vector", lambda e, sC=sC, j=j, qsl=qsl:
                            e.tensor_scalar(sC[:], pm[qsl][:], b_qkv[:, j:j + 1],
                                            0.0, ALU.add, ALU.min), sem="dve",
                            writes=(tC,))
                        P.emit("scalar", lambda e, sC=sC, sE=sE: e.activation(
                            sE[:], sC[:], AF.Exp), sem="act",
                            reads=(tC,), writes=(tE,))
                        ev_qj = P.emit("vector", lambda e, sD=sD, sE=sE, j=j:
                                       e.tensor_add(q_fm[:, j * SC:(j + 1) * SC],
                                                    sD[:], sE[:]), sem="dve",
                                       waits=[ev_store["q"]], reads=(tD, tE))
                        return ev_qj

                    ev_kj = ev_qj = None
                    for j in range(8):
                        ev_kj = emit_gate_k(j)
                        if j >= 1:
                            ev_qj = emit_q(j - 1)
                        if j == 5 and g + 1 < NSC:
                            evn = emit_ln12(g + 1)
                        ev_gate_dbg = None
                    ev_qj = emit_q(7)
                    ev_vv = []
                    for c in range(CPS):
                        s0 = slot_rr[0]; slot_rr[0] = (s0 + 1) % 4
                        s1 = slot_rr[0]; slot_rr[0] = (s1 + 1) % 4
                        P.waits("tensor", [ev_xn[-1], ev_w1, ev_pm_free[s0],
                                           ev_pm_free[s1]])
                        ev0 = ev1 = None
                        for kk in range(8):
                            ev0 = P.emit("tensor", lambda e, c=c, kk=kk, s0=s0: e.matmul(
                                pm[s0][:],
                                xn_fm[:, kk * SC + c * 128:kk * SC + (c + 1) * 128],
                                w_qkv_b[:, kk * 3072 + 2048:kk * 3072 + 2560],
                                start=(kk == 0), stop=(kk == 7)), sem="mm")
                            ev1 = P.emit("tensor", lambda e, c=c, kk=kk, s1=s1: e.matmul(
                                pm[s1][:],
                                xn_fm[:, kk * SC + c * 128:kk * SC + (c + 1) * 128],
                                w_qkv_b[:, kk * 3072 + 2560:kk * 3072 + 3072],
                                start=(kk == 0), stop=(kk == 7)), sem="mm")
                        P.wait("vector", ev1)
                        P.wait("vector", ev_store["v"])
                        ev_pm_free[s0] = P.emit("vector", lambda e, c=c, s0=s0:
                                                e.tensor_add(
                                                    v_tk[:, c * 1024:c * 1024 + 512],
                                                    pm[s0][:], b_v_t[:, 0:512]),
                                                sem="dve")
                        ev_pm_free[s1] = P.emit("vector", lambda e, c=c, s1=s1:
                                                e.tensor_add(
                                                    v_tk[:, c * 1024 + 512:
                                                         (c + 1) * 1024],
                                                    pm[s1][:], b_v_t[:, 512:1024]),
                                                sem="dve")
                        ev_vv.append(ev_pm_free[s1])
                    ev_mm_last = ev1
                    if dev_scratch:
                        P.waits("sync", [ev_xn[-1], ev_gate_dbg])
                        P.emit("sync", lambda e, g=g: e.dma_start(
                            out=dbg_xn[g], in_=xn_fm[:]), sem="so", amount=16)
                        P.emit("sync", lambda e, g=g: e.dma_start(
                            out=dbg_gate[g], in_=scr[5][:]), sem="so", amount=16)
                        P.emit("sync", lambda e, g=g: e.dma_start(
                            out=dbg_xntok[g], in_=xn_tok[:]), sem="so", amount=16)
                        P.emit("sync", lambda e, g=g: e.dma_start(
                            out=dbg_stats[g], in_=stats[:]), sem="so", amount=16)
                        if g == 0:
                            P.emit("sync", lambda e: e.dma_start(
                                out=dbg_ident, in_=ident[:]), sem="so", amount=16)
                    ev_q_all = ev_qj
                    ev_k_all = ev_kj
                    P.wait("sync", ev_q_all)
                    ev_store["q"] = P.emit("sync", lambda e, g=g: e.dma_start(
                        out=q_sc[g], in_=q_fm[:]), sem="sq", amount=16)
                    P.wait("sync", ev_k_all)
                    ev_store["k"] = P.emit("sync", lambda e, g=g: e.dma_start(
                        out=k_sc[g], in_=k_fm[:]), sem="sk", amount=16)
                    evkp = None
                    for j in range(8):
                        evkp = P.emit("sync", lambda e, g=g, j=j: e.dma_start(
                            out=k_pl[j * 128:(j + 1) * 128, g * SC:(g + 1) * SC],
                            in_=k_fm[:, j * SC:(j + 1) * SC]), sem="skp", amount=16)
                    ev_store["kp"] = evkp
                    evv = None
                    for c in range(CPS):
                        P.wait("sync", ev_vv[c])
                        evv = P.emit("sync", lambda e, g=g, c=c: e.dma_start(
                            out=v_pl[g * SC + c * 128:g * SC + (c + 1) * 128, :],
                            in_=v_tk[:, c * 1024:(c + 1) * 1024]), sem="sv", amount=16)
                    ev_store["v"] = evv
                p1_done = P.snapshot()
            P.run_block()

        if 2 in phases:
            with ExitStack() as p2:
                def sb2(name, shape, dtype):
                    return p2.enter_context(nc.sbuf_tensor(name, shape, dtype))
                S_f = sb2("S_f", [128, 8 * 1025], F32)
                S_b = sb2("S_b", [128, 8 * 1025], BF16)
                w_proj_b = sb2("w_proj_b", [128, 8 * 1024], BF16)
                q_sb = [sb2(f"q_sb{i}", [128, 8 * SC], BF16) for i in range(2)]
                k_sb = [sb2(f"k_sb{i}", [128, 8 * SC], BF16) for i in range(2)]
                kT_sb = [sb2(f"kT_sb{i}", [128, CPS * 1024], BF16) for i in range(2)]
                v_sb = [sb2(f"v_sb{i}", [128, CPS * 1025], BF16) for i in range(2)]
                x_sb = [sb2(f"x_sb{i}", [128, CPS * 1024], F32) for i in range(2)]
                aT_sb = sb2("aT_sb", [128, 2 * CPS * 128], BF16)
                odiv_tok = sb2("odiv_tok", [128, 1024], BF16)
                odiv_fm = sb2("odiv_fm", [128, 1024], BF16)
                ao_sb = sb2("ao_sb", [128, 1024], F32)
                t3_sb = sb2("t3_sb", [128, 1024], F32)
                xn4_tok = [sb2(f"xn4_{i}", [128, 1024], BF16) for i in range(2)]
                pA = p2.enter_context(nc.psum_tensor("pA", [128, 1280], F32))
                pO = p2.enter_context(nc.psum_tensor("pO", [128, 1280], F32))
                pT2 = p2.enter_context(nc.psum_tensor("pT2", [128, 2048], BF16))

                if 1 in phases:
                    P.barrier(p1_done)
                ev_Sf0 = P.emit("gpsimd", lambda e: e.memset(S_f[:], 0.0), sem="pool")
                ev_S0 = P.emit("gpsimd", lambda e: e.memset(S_b[:], 0.0), sem="pool")
                ev_vones = None
                for par in range(2):
                    for c in range(CPS):
                        ev_vones = P.emit("gpsimd", lambda e, par=par, c=c: e.memset(
                            v_sb[par][:, c * 1025 + 1024:(c + 1) * 1025], 1.0), sem="pool")
                ev_wp = None
                P.wait("sync", ev_cast_wp)
                for kk in range(8):
                    ev_wp = P.emit("sync", lambda e, kk=kk: e.dma_start(
                        out=w_proj_b[:, kk * 1024:(kk + 1) * 1024],
                        in_=wp_bf[kk * 128:(kk + 1) * 128, :]), sem="dw", amount=16)

                ev_ld = [None, None]
                ev_ldT = [None, None]
                ev_buf_free = [None, None]
                ev_S_cast = ev_S0
                ev_S_read = None
                ev_pA_free = [None, None]
                ev_pO_free = None
                ev_pT2_free = [None, None]
                ev_odiv_free = None
                ev_x1st = None
                ev_xn4st = [None, None]
                ev_aT_free = [[None] * CPS, [None] * CPS]
                ev_pairs = {}

                def load_sc(g):
                    par = g % 2
                    if ev_buf_free[par] is not None:
                        P.waits("sync", ev_buf_free[par])
                    evs = []
                    dsem = "da" if par == 0 else "db"
                    evs.append(P.emit("sync", lambda e, g=g, par=par: e.dma_start(
                        out=q_sb[par][:], in_=q_sc[g]), sem=dsem, amount=16))
                    evs.append(P.emit("sync", lambda e, g=g, par=par: e.dma_start(
                        out=k_sb[par][:], in_=k_sc[g]), sem=dsem, amount=16))
                    for c in range(CPS):
                        evs.append(P.emit("sync", lambda e, g=g, c=c, par=par: e.dma_start(
                            out=v_sb[par][:, c * 1025:c * 1025 + 1024],
                            in_=v_pl[g * SC + c * 128:g * SC + (c + 1) * 128, :]),
                            sem=dsem, amount=16))
                        evs.append(P.emit("sync", lambda e, g=g, c=c, par=par: e.dma_start(
                            out=x_sb[par][:, c * 1024:(c + 1) * 1024],
                            in_=x_d[g * SC + c * 128:g * SC + (c + 1) * 128, :]),
                            sem=dsem, amount=16))
                    ev_ld[par] = evs[-1]
                    if ev_buf_free[par] is not None:
                        P.waits("scalar", ev_buf_free[par])
                    evT = None
                    for c in range(CPS):
                        evT = P.emit("scalar", lambda e, g=g, c=c, par=par: e.dma_start(
                            out=kT_sb[par][:, c * 1024:(c + 1) * 1024],
                            in_=k_pl[:, (g * CPS + c) * 128:(g * CPS + c + 1) * 128],
                            transpose=True), sem="dta" if par == 0 else "dtb",
                            amount=16)
                    ev_ldT[par] = evT

                def emit_pairs(t):
                    g, c = t // CPS, t % CPS
                    par = g % 2
                    pp = (t % 2) * CPS * 128
                    ks, qs = k_sb[par], q_sb[par]
                    out = []
                    for cp in range(c + 1):
                        sl = cp % 2
                        off = 0 if sl == 0 else 512
                        P.waits("tensor", [ev_ld[par], ev_pA_free[sl]])
                        evt = None
                        for kk in range(8):
                            evt = P.emit("tensor",
                                         lambda e, kk=kk, cp=cp, c=c, off=off,
                                         ks=ks, qs=qs:
                                         e.matmul(pA[:, off:off + 128],
                                                  ks[:, kk * SC + cp * 128:
                                                     kk * SC + (cp + 1) * 128],
                                                  qs[:, kk * SC + c * 128:
                                                     kk * SC + (c + 1) * 128],
                                                  start=(kk == 0), stop=(kk == 7)),
                                         sem="mm" if kk == 7 else None)
                        if cp == c:
                            ev = P.emit("vector", lambda e, cp=cp, off=off, pp=pp:
                                        e.tensor_mul(
                                            aT_sb[:, pp + cp * 128:pp + (cp + 1) * 128],
                                            pA[:, off:off + 128],
                                            mask_keep[:]), sem="dve",
                                        waits=[evt, ev_mask,
                                               ev_aT_free[t % 2][cp]])
                        else:
                            ev = P.emit("scalar", lambda e, cp=cp, off=off, pp=pp:
                                        e.activation(
                                            aT_sb[:, pp + cp * 128:pp + (cp + 1) * 128],
                                            pA[:, off:off + 128],
                                            AF.Identity), sem="act",
                                        waits=[evt, ev_aT_free[t % 2][cp]])
                        ev_pA_free[sl] = ev
                        out.append(ev)
                    return out

                load_sc(0)
                ev_pairs[0] = emit_pairs(0)
                for g in range(NSC):
                    par = g % 2
                    if g + 1 < NSC:
                        load_sc(g + 1)
                    qs, ks, kTs, vs, xs = (q_sb[par], k_sb[par], kT_sb[par],
                                           v_sb[par], x_sb[par])
                    for c in range(CPS):
                        tglob = g * CPS + c
                        pp = (tglob % 2) * CPS * 128
                        evacs = ev_pairs.pop(tglob)
                        n_per = {0: (c + 1) + (8 if g > 0 else 0)}
                        idx = [0]
                        P.waits("tensor", [ev_pO_free, ev_ld[par], ev_vones])
                        ev_av_last = None
                        for cp in range(c + 1):
                            P.wait("tensor", evacs[cp])
                            for (o0, w) in ((0, 512), (512, 512), (1024, 1)):
                                st = idx[0] == 0
                                sp = (g == 0 and cp == c)
                                if o0 == 1024:
                                    idx[0] += 1
                                ev_av_last = P.emit(
                                    "tensor",
                                    lambda e, cp=cp, o0=o0, w=w, st=st, sp=sp,
                                    vs=vs, pp=pp:
                                    e.matmul(
                                        pO[:, o0:o0 + w],
                                        aT_sb[:, pp + cp * 128:pp + (cp + 1) * 128],
                                        vs[:, cp * 1025 + o0:cp * 1025 + o0 + w],
                                        start=st, stop=sp),
                                    sem="mm" if (cp == c and o0 == 1024) else None)
                        ev_cross_last = ev_av_last
                        if g > 0:
                            P.waits("tensor", [ev_S_cast])
                            for j in range(8):
                                for (o0, w) in ((0, 512), (512, 512), (1024, 1)):
                                    ev_cross_last = P.emit(
                                        "tensor",
                                        lambda e, j=j, c=c, o0=o0, w=w, qs=qs:
                                        e.matmul(
                                            pO[:, o0:o0 + w],
                                            qs[:, j * SC + c * 128:j * SC + (c + 1) * 128],
                                            S_b[:, j * 1025 + o0:j * 1025 + o0 + w],
                                            start=False, stop=(j == 7)),
                                        sem="mm" if (j == 7 and o0 == 1024) else None)
                            ev_S_read = ev_cross_last
                        ev_av_last = ev_cross_last
                        for cp in range(c + 1):
                            ev_aT_free[tglob % 2][cp] = ev_av_last
                        if tglob + 1 < NCH:
                            ev_pairs[tglob + 1] = emit_pairs(tglob + 1)
                        cst = c % 4
                        ev_d1 = P.emit("vector", lambda e, cst=cst: e.tensor_scalar(
                            st_den[:, cst:cst + 1], pO[:, 1024:1025], DEN_EPS, None,
                            ALU.add), sem="dve", waits=[ev_av_last],
                            writes=(f"dn{cst}",))
                        ev_d2 = P.emit("vector", lambda e, cst=cst: e.reciprocal(
                            st_rc[:, cst:cst + 1], st_den[:, cst:cst + 1]), sem="dve",
                            reads=(f"dn{cst}",), writes=(f"rc{cst}",))
                        ev_odiv = P.emit("vector", lambda e, cst=cst: e.tensor_scalar(
                            odiv_tok[:], pO[:, 0:1024], st_rc[:, cst:cst + 1], None,
                            ALU.mult), sem="dve",
                            waits=[ev_pT2_free[0], ev_pT2_free[1]],
                            reads=(f"rc{cst}",))
                        tp = c % 2
                        P.waits("tensor", [ev_odiv, ev_pT2_free[tp]])
                        evt = None
                        for j in range(8):
                            evt = P.emit("tensor", lambda e, j=j, tp=tp: e.transpose(
                                pT2[:, tp * 1024 + j * 128:tp * 1024 + (j + 1) * 128],
                                odiv_tok[:, j * 128:(j + 1) * 128], ident[:]), sem="mm")
                        P.waits("scalar", [evt, ev_odiv_free])
                        ev_fm = P.emit("scalar", lambda e, tp=tp: e.activation(
                            odiv_fm[:], pT2[:, tp * 1024:(tp + 1) * 1024], AF.Identity),
                            sem="act")
                        ev_pT2_free[tp] = ev_fm
                        P.waits("tensor", [ev_fm, ev_wp, ev_odiv])
                        ev_pj = None
                        for j in range(8):
                            P.emit("tensor", lambda e, j=j: e.matmul(
                                pO[:, 0:512], odiv_fm[:, j * 128:(j + 1) * 128],
                                w_proj_b[:, j * 1024:j * 1024 + 512],
                                start=(j == 0), stop=(j == 7)), sem="mm")
                            ev_pj = P.emit("tensor", lambda e, j=j: e.matmul(
                                pO[:, 512:1024], odiv_fm[:, j * 128:(j + 1) * 128],
                                w_proj_b[:, j * 1024 + 512:(j + 1) * 1024],
                                start=(j == 0), stop=(j == 7)), sem="mm")
                        ev_odiv_free = ev_pj
                        ev_ao = P.emit("vector", lambda e: e.tensor_add(
                            ao_sb[:], pO[:, 0:1024], b_proj_t[:]), sem="dve",
                            waits=[ev_pj], writes=("ao",))
                        ev_pO_free = ev_ao
                        r3, n3, evn3, _ = ln_stats(4 + cst, ao_sb[:], ev_ao, 1.0, EPS,
                                                   xtok="ao")
                        x1ap = xs[:, c * 1024:(c + 1) * 1024]
                        ev_t3 = P.emit("vector", lambda e, r3=r3, n3=n3: e.tensor_scalar(
                            t3_sb[:], ao_sb[:], r3, n3, ALU.mult, ALU.add), sem="dve",
                            waits=[evn3],
                            reads=("ao", f"r{4 + cst}", f"n{4 + cst}"), writes=("t3",))
                        ev_x1 = P.emit("vector", lambda e, x1ap=x1ap: e.tensor_add(
                            x1ap, x1ap, t3_sb[:]), sem="dve",
                            waits=[ev_ld[par], ev_x1st], reads=("t3",))
                        r4, n4, evn4, ev_r4 = ln_stats(8 + cst, x1ap, ev_x1, 1.0, EPS)
                        ev_xn4 = P.emit("scalar", lambda e, x1ap=x1ap, r4=r4, n4=n4, tp=tp:
                                        e.activation(xn4_tok[tp][:], x1ap, AF.Identity,
                                                     bias=n4, scale=r4), sem="act",
                                        waits=[evn4, ev_xn4st[tp]],
                                        reads=(f"r{8 + cst}", f"n{8 + cst}"))
                        P.wait("sync", ev_x1)
                        ev_x1st = P.emit("sync", lambda e, tglob=tglob, x1ap=x1ap:
                                         e.dma_start(out=x1_sc[tglob * 128:
                                                              (tglob + 1) * 128, :],
                                                     in_=x1ap), sem="sx1", amount=16)
                        P.wait("sync", ev_xn4)
                        ev_xn4st[tp] = P.emit("sync", lambda e, tglob=tglob, tp=tp:
                                              e.dma_start(out=xn4_sc[tglob * 128:
                                                                     (tglob + 1) * 128, :],
                                                          in_=xn4_tok[tp][:]),
                                              sem="sxn0" if tp == 0 else "sxn1",
                                              amount=16)
                    ev_ds_ev = None
                    for j in range(8):
                        if j % 2 == 0:
                            reg = pO
                            P.waits("tensor", [ev_pO_free])
                        else:
                            reg = pA
                            P.waits("tensor", [ev_pA_free[0], ev_pA_free[1]])
                        P.waits("tensor", [ev_ldT[par], ev_ld[par], ev_vones])
                        evm = None
                        for c in range(CPS):
                            for (o0, w) in ((0, 512), (512, 512), (1024, 1)):
                                evm = P.emit("tensor",
                                             lambda e, j=j, c=c, o0=o0, w=w, reg=reg,
                                             kTs=kTs, vs=vs:
                                             e.matmul(reg[:, o0:o0 + w],
                                                      kTs[:, c * 1024 + j * 128:
                                                          c * 1024 + (j + 1) * 128],
                                                      vs[:, c * 1025 + o0:
                                                         c * 1025 + o0 + w],
                                                      start=(c == 0), stop=(c == CPS - 1)),
                                             sem="mm")
                        ev_dve = P.emit("vector", lambda e, j=j, reg=reg: e.tensor_add(
                            S_f[:, j * 1025:(j + 1) * 1025],
                            S_f[:, j * 1025:(j + 1) * 1025],
                            reg[:, 0:1025]), sem="dve",
                            waits=[evm, ev_S_read, ev_Sf0],
                            reads=(f"Sf{j}",), writes=(f"Sf{j}",))
                        if j % 2 == 0:
                            ev_pO_free = ev_dve
                        else:
                            ev_pA_free[0] = ev_dve
                            ev_pA_free[1] = ev_dve
                        ev_ds_ev = P.emit("gpsimd", lambda e, j=j: e.tensor_copy(
                            S_b[:, j * 1025:(j + 1) * 1025],
                            S_f[:, j * 1025:(j + 1) * 1025]), sem="pool",
                            reads=(f"Sf{j}",))
                    ev_S_cast = ev_ds_ev
                    ev_buf_free[par] = [evm, ev_x1st]
                p2_done = P.snapshot()
            P.run_block()

        if 4 in phases:
            with ExitStack() as p4:
                def sb4(name, shape, dtype):
                    return p4.enter_context(nc.sbuf_tensor(name, shape, dtype))
                wg_b = sb4("wg_b", [128, 8 * 3072], BF16)
                wu_b = sb4("wu_b", [128, 8 * 3072], BF16)
                xn4_fm = [sb4(f"xn4fm{i}", [128, 8 * 512], BF16) for i in range(2)]
                h_sb = [sb4(f"h_sb{i}", [128, 24 * 512], BF16) for i in range(2)]
                silu_s = [sb4(f"silu{i}", [128, 512], F32) for i in range(2)]
                pg = [p4.enter_context(nc.psum_tensor(f"pg{i}", [128, 512], F32))
                      for i in range(2)]
                pu = [p4.enter_context(nc.psum_tensor(f"pu{i}", [128, 512], F32))
                      for i in range(2)]

                if 2 in phases:
                    P.barrier(p2_done)
                P.wait("sync", ev_cast_wg)
                P.wait("sync", ev_cast_wu)
                ev_wg4 = None
                for kk in range(8):
                    ev_wg4 = P.emit("sync", lambda e, kk=kk: e.dma_start(
                        out=wg_b[:, kk * 3072:(kk + 1) * 3072],
                        in_=wg_bf[kk * 128:(kk + 1) * 128, :]), sem="lwg", amount=16)
                P.wait("gpsimd", ev_cast_wu)
                ev_w4 = None
                for kk in range(8):
                    ev_w4 = P.emit("gpsimd", lambda e, kk=kk: e.dma_start(
                        out=wu_b[:, kk * 3072:(kk + 1) * 3072],
                        in_=wu_bf[kk * 128:(kk + 1) * 128, :]), sem="lwu", amount=16)

                ev_xnl = [None, None]
                ev_xn_free = [None, None]
                ev_pg_free = [None, None]
                ev_pu_free = [None, None]
                ev_silu_free = [None, None]
                ev_hst = [None, None]

                def load_xn4(t):
                    par = t % 2
                    P.wait("scalar", ev_xn_free[par])
                    ev = None
                    for j in range(8):
                        ev = P.emit("scalar", lambda e, t=t, j=j, par=par: e.dma_start(
                            out=xn4_fm[par][:, j * 512:(j + 1) * 512],
                            in_=xn4_sc[t * 512:(t + 1) * 512, j * 128:(j + 1) * 128],
                            transpose=True), sem="dta" if par == 0 else "dtb",
                            amount=16)
                    ev_xnl[par] = ev

                load_xn4(0)
                for t in range(NT5):
                    par = t % 2
                    if t + 1 < NT5:
                        load_xn4(t + 1)
                    ev_mm_xn = None
                    ev_h_last = None
                    for hc in range(24):
                        sl = hc % 2
                        P.waits("tensor", [ev_xnl[par], ev_wg4, ev_pg_free[sl],
                                           ev_pu_free[sl]])
                        evg = evu = None
                        for kk in range(8):
                            evg = P.emit("tensor", lambda e, hc=hc, kk=kk, sl=sl, par=par:
                                         e.matmul(pg[sl][:],
                                                  wg_b[:, kk * 3072 + hc * 128:
                                                       kk * 3072 + (hc + 1) * 128],
                                                  xn4_fm[par][:, kk * 512:(kk + 1) * 512],
                                                  start=(kk == 0), stop=(kk == 7)),
                                         sem="mm")
                        P.wait("tensor", ev_w4)
                        for kk in range(8):
                            evu = P.emit("tensor", lambda e, hc=hc, kk=kk, sl=sl, par=par:
                                         e.matmul(pu[sl][:],
                                                  wu_b[:, kk * 3072 + hc * 128:
                                                       kk * 3072 + (hc + 1) * 128],
                                                  xn4_fm[par][:, kk * 512:(kk + 1) * 512],
                                                  start=(kk == 0), stop=(kk == 7)),
                                         sem="mm")
                        ev_mm_xn = evu
                        P.waits("scalar", [evg, ev_silu_free[sl]])
                        ev_sl = P.emit("scalar", lambda e, sl=sl: e.activation(
                            silu_s[sl][:], pg[sl][:], AF.Silu, scale=QSCALE), sem="act")
                        ev_pg_free[sl] = ev_sl
                        P.waits("vector", [ev_sl, evu, ev_hst[par]])
                        ev_h = P.emit("vector", lambda e, hc=hc, sl=sl, par=par:
                                      e.scalar_tensor_tensor(
                                          h_sb[par][:, hc * 512:(hc + 1) * 512],
                                          pu[sl][:], QSCALE, silu_s[sl][:],
                                          ALU.mult, ALU.mult), sem="dve")
                        ev_pu_free[sl] = ev_h
                        ev_silu_free[sl] = ev_h
                        ev_h_last = ev_h
                    ev_xn_free[par] = ev_mm_xn
                    P.wait("sync", ev_h_last)
                    ev_hst[par] = P.emit("sync", lambda e, t=t, par=par: e.dma_start(
                        out=h_sc[t], in_=h_sb[par][:]),
                        sem="sh0" if par == 0 else "sh1", amount=16)
                p4_done = P.snapshot()
            P.run_block()

        if 5 in phases:
            with ExitStack() as p5:
                def sb5(name, shape, dtype):
                    return p5.enter_context(nc.sbuf_tensor(name, shape, dtype))
                wd_b = sb5("wd_b", [128, 24 * 1024], BF16)
                h5 = [sb5(f"h5_{i}", [128, 24 * 512], BF16) for i in range(2)]
                x15 = [sb5(f"x15_{i}", [128, 4 * 1024], F32) for i in range(2)]
                y_sb = sb5("y_sb", [128, 1024], F32)
                t5_sb = sb5("t5_sb", [128, 1024], F32)
                py = [p5.enter_context(nc.psum_tensor(f"py{i}", [128, 1024], F32))
                      for i in range(2)]

                if 4 in phases:
                    P.barrier(p4_done)
                ev_w5 = None
                P.wait("sync", ev_cast_wd)
                P.wait("gpsimd", ev_cast_wd)
                for kk in range(24):
                    eng5 = "sync" if kk % 2 == 0 else "gpsimd"
                    sem5 = "lwd" if kk % 2 == 0 else "dwp"
                    ev_w5a = P.emit(eng5, lambda e, kk=kk: e.dma_start(
                        out=wd_b[:, kk * 1024:(kk + 1) * 1024],
                        in_=wd_bf[kk * 128:(kk + 1) * 128, :]), sem=sem5, amount=16)
                    if kk % 2 == 0:
                        ev_w5 = ev_w5a
                    else:
                        ev_w5b = ev_w5a

                ev_l5 = [None, None]
                ev_b5_free = [None, None]
                ev_py_free = [None, None]
                ev_y_free = None
                ev_ost = None

                def load5(t):
                    par = t % 2
                    if ev_b5_free[par] is not None:
                        P.waits("sync", ev_b5_free[par])
                    dsem = "da" if par == 0 else "db"
                    P.emit("sync", lambda e, t=t, par=par: e.dma_start(
                        out=h5[par][:], in_=h_sc[t]), sem=dsem, amount=16)
                    evs = None
                    for c in range(4):
                        evs = P.emit("sync", lambda e, t=t, c=c, par=par: e.dma_start(
                            out=x15[par][:, c * 1024:(c + 1) * 1024],
                            in_=x1_sc[t * 512 + c * 128:t * 512 + (c + 1) * 128, :]),
                            sem=dsem, amount=16)
                    ev_l5[par] = evs

                load5(0)
                for t in range(NT5):
                    par = t % 2
                    if t + 1 < NT5:
                        load5(t + 1)
                    ev_mm5 = None
                    for c in range(4):
                        sl = c % 2
                        P.waits("tensor", [ev_l5[par], ev_w5, ev_w5b, ev_py_free[sl]])
                        evm = None
                        for hc in range(24):
                            P.emit("tensor", lambda e, hc=hc, c=c, sl=sl, par=par:
                                   e.matmul(py[sl][:, 0:512],
                                            h5[par][:, hc * 512 + c * 128:
                                                    hc * 512 + (c + 1) * 128],
                                            wd_b[:, hc * 1024:hc * 1024 + 512],
                                            start=(hc == 0), stop=(hc == 23)), sem="mm")
                            evm = P.emit("tensor", lambda e, hc=hc, c=c, sl=sl, par=par:
                                         e.matmul(py[sl][:, 512:1024],
                                                  h5[par][:, hc * 512 + c * 128:
                                                          hc * 512 + (c + 1) * 128],
                                                  wd_b[:, hc * 1024 + 512:
                                                       (hc + 1) * 1024],
                                                  start=(hc == 0), stop=(hc == 23)),
                                         sem="mm")
                        ev_mm5 = evm
                        cst = c % 4
                        ev_y = P.emit("scalar", lambda e, sl=sl: e.activation(
                            y_sb[:], py[sl][:], AF.Identity, scale=QSCALE), sem="act",
                            waits=[evm, ev_y_free], writes=("y5",))
                        ev_py_free[sl] = ev_y
                        r5, n5, evn5, _ = ln_stats(12 + cst, y_sb[:], ev_y, 1.0, EPS,
                                                   xtok="y5")
                        ev_t5 = P.emit("vector", lambda e, r5=r5, n5=n5: e.tensor_scalar(
                            t5_sb[:], y_sb[:], r5, n5, ALU.mult, ALU.add), sem="dve",
                            waits=[evn5],
                            reads=("y5", f"r{12 + cst}", f"n{12 + cst}"), writes=("t5",))
                        ev_y_free = ev_t5
                        x1ap = x15[par][:, c * 1024:(c + 1) * 1024]
                        ev_o = P.emit("vector", lambda e, x1ap=x1ap: e.tensor_add(
                            x1ap, x1ap, t5_sb[:]), sem="dve", reads=("t5",))
                        P.wait("sync", ev_o)
                        ev_ost = P.emit("sync", lambda e, t=t, c=c, x1ap=x1ap:
                                        e.dma_start(out=out_d[t * 512 + c * 128:
                                                              t * 512 + (c + 1) * 128, :],
                                                    in_=x1ap), sem="so", amount=16)
                    ev_b5_free[par] = [ev_mm5, ev_ost]
                P.wait("sync", ev_ost)
            P.run_block()
        else:
            pass

    return nc


_CACHE = {}


def kernel(L0_attn_x, L0_attn_w_qkv, L0_attn_b_qkv, L0_attn_w_gate, L0_attn_b_gate,
           L0_attn_w_proj, L0_attn_b_proj, L0_attn_ln_g, L0_attn_ln_b,
           L0_ffn_w_gate, L0_ffn_w_up, L0_ffn_w_down):
    x = np.asarray(L0_attn_x, dtype=np.float32)
    B, T, _ = x.shape
    w_qkv = np.ascontiguousarray(np.asarray(L0_attn_w_qkv, dtype=np.float32))
    w_gate = np.ascontiguousarray(np.asarray(L0_attn_w_gate, dtype=np.float32))
    w_proj = np.ascontiguousarray(np.asarray(L0_attn_w_proj, dtype=np.float32))
    b_qkv = np.asarray(L0_attn_b_qkv, dtype=np.float32)
    b_gate = np.asarray(L0_attn_b_gate, dtype=np.float32)
    b_proj = np.asarray(L0_attn_b_proj, dtype=np.float32)
    ln_g = np.asarray(L0_attn_ln_g, dtype=np.float32)
    ln_b = np.asarray(L0_attn_ln_b, dtype=np.float32)
    wg_t = np.ascontiguousarray(np.asarray(L0_ffn_w_gate, dtype=np.int32).T)
    wu_t = np.ascontiguousarray(np.asarray(L0_ffn_w_up, dtype=np.int32).T)
    wd_t = np.ascontiguousarray(np.asarray(L0_ffn_w_down, dtype=np.int32).T)

    b_qkv_fm = np.ascontiguousarray(b_qkv.reshape(24, 128).T)
    b_gate_fm = np.ascontiguousarray(b_gate.reshape(8, 128).T)
    ln_g_fm = np.ascontiguousarray(ln_g.reshape(8, 128).T)
    ln_b_fm = np.ascontiguousarray(ln_b.reshape(8, 128).T)
    b_proj_t = np.ascontiguousarray(np.broadcast_to(b_proj, (128, D)))
    b_v_t = np.ascontiguousarray(np.broadcast_to(b_qkv[2 * D:3 * D], (128, D)))

    if T not in _CACHE:
        _CACHE[T] = build_nc(T)
    nc = _CACHE[T]
    shared = dict(w_qkv=w_qkv, w_gate=w_gate, w_proj=w_proj, b_qkv_fm=b_qkv_fm,
                  b_gate_fm=b_gate_fm, b_proj_t=b_proj_t, b_v_t=b_v_t,
                  ln_g_fm=ln_g_fm, ln_b_fm=ln_b_fm, wg_t=wg_t, wu_t=wu_t, wd_t=wd_t)
    in_maps = [dict(shared, x=np.ascontiguousarray(x[i])) for i in range(B)]
    res = run_bass_kernel_spmd(nc, in_maps, core_ids=list(range(B)))
    return np.stack([np.asarray(res.results[i]["out"]) for i in range(B)])


# version 17
# speedup vs baseline: 1.0492x; 1.0476x over previous
import numpy as np
from contextlib import ExitStack

import concourse.bass as bass
import concourse.mybir as mybir
from concourse.bass_utils import run_bass_kernel_spmd

F32 = mybir.dt.float32
BF16 = mybir.dt.bfloat16
I32 = mybir.dt.int32
AF = mybir.ActivationFunctionType
ALU = mybir.AluOpType
AX = mybir.AxisListType

D = 1024
HID = 3072
NCORE = 8
EPS = 1e-5
DEN_EPS = 1e-6
QSCALE = 1.0 / 127.0


class Prog:

    ENGINES = ("sync", "gpsimd", "scalar", "vector", "tensor")

    def __init__(self, nc, ctx):
        self.nc = nc
        self.cnt = {}
        self.h = {}
        for name in ("mm", "dve", "act", "pool", "da", "db", "dta", "dtb", "dw",
                     "dwp", "cwp", "cwg", "cwu", "cwd", "lwp", "lwg", "lwu", "lwd",
                     "sq", "sk", "skp", "sv", "sx1", "sxn0", "sxn1",
                     "sh0", "sh1", "so"):
            self.h[name] = ctx.enter_context(nc.semaphore(name))
            self.cnt[name] = 0
        self.track = {}
        self._reset()

    def _reset(self):
        self.ops = {e: [] for e in self.ENGINES}
        self.lastwait = {}

    def inc(self, name, amount=1):
        self.cnt[name] += amount
        return (name, self.cnt[name])

    def wait(self, eng, ev):
        if ev is None:
            return
        name, val = ev
        if val <= 0:
            return
        key = (eng, name)
        if self.lastwait.get(key, -1) >= val:
            return
        self.lastwait[key] = val
        h = self.h[name]
        self.ops[eng].append(lambda e, h=h, v=val: e.wait_ge(h, v))

    def waits(self, eng, evs):
        for ev in evs:
            self.wait(eng, ev)

    def emit(self, eng, fn, sem=None, amount=1, reads=(), writes=(), waits=()):
        for ev in waits:
            self.wait(eng, ev)
        for t in reads:
            rec = self.track.get(t)
            if rec is not None and rec[0] is not None:
                self.wait(eng, rec[0][0])
        for t in writes:
            rec = self.track.get(t)
            if rec is not None:
                w = rec[0]
                if w is not None and not (eng == "tensor" and w[1] == "tensor"
                                          and t.startswith("ps:")):
                    self.wait(eng, w[0])
                for rev, _ in rec[1]:
                    self.wait(eng, rev)
        if sem is not None:
            ev = self.inc(sem, amount)
            h = self.h[sem]
            self.ops[eng].append(lambda e, fn=fn, h=h, a=amount: fn(e).then_inc(h, a))
        else:
            assert not reads and not writes, "tracked ops need a sem"
            self.ops[eng].append(fn)
            return None
        for t in reads:
            self.track.setdefault(t, [None, []])[1].append((ev, eng))
        for t in writes:
            self.track[t] = [(ev, eng), []]
        return ev

    def snapshot(self):
        return dict(self.cnt)

    def barrier(self, snap):
        for eng in self.ENGINES:
            for name, val in snap.items():
                self.wait(eng, (name, val))

    def run_block(self):
        with self.nc.Block() as block:
            for eng in self.ENGINES:
                ops = self.ops[eng]
                getattr(block, eng)(lambda e, ops=ops: [op(e) for op in ops])
        self._reset()


def build_nc(T=4096, phases=(1, 2, 4, 5), dev_scratch=False):
    nc = bass.Bass("TRN2", target_bir_lowering=False)
    NCH = T // 128
    SC = 512
    CPS = SC // 128
    NSC = T // SC
    NT5 = T // 512

    x_d = nc.dram_tensor("x", [T, D], F32, kind="ExternalInput").ap()
    w_qkv_d = nc.dram_tensor("w_qkv", [D, 3 * D], F32, kind="ExternalInput").ap()
    w_gate_d = nc.dram_tensor("w_gate", [D, D], F32, kind="ExternalInput").ap()
    w_proj_d = nc.dram_tensor("w_proj", [D, D], F32, kind="ExternalInput").ap()
    b_qkv_d = nc.dram_tensor("b_qkv_fm", [128, 24], F32, kind="ExternalInput").ap()
    b_gate_d = nc.dram_tensor("b_gate_fm", [128, 8], F32, kind="ExternalInput").ap()
    b_proj_d = nc.dram_tensor("b_proj_t", [128, D], F32, kind="ExternalInput").ap()
    b_v_d = nc.dram_tensor("b_v_t", [128, D], F32, kind="ExternalInput").ap()
    ln_g_d = nc.dram_tensor("ln_g_fm", [128, 8], F32, kind="ExternalInput").ap()
    ln_b_d = nc.dram_tensor("ln_b_fm", [128, 8], F32, kind="ExternalInput").ap()
    wg_d = nc.dram_tensor("wg_t", [D, HID], I32, kind="ExternalInput").ap()
    wu_d = nc.dram_tensor("wu_t", [D, HID], I32, kind="ExternalInput").ap()
    wd_d = nc.dram_tensor("wd_t", [HID, D], I32, kind="ExternalInput").ap()
    out_d = nc.dram_tensor("out", [T, D], F32, kind="ExternalOutput").ap()

    skind = "ExternalOutput" if dev_scratch else "Internal"
    if dev_scratch:
        dbg_xn = nc.dram_tensor("dbg_xn", [NSC, 128, 8 * SC], BF16, kind=skind).ap()
        dbg_gate = nc.dram_tensor("dbg_gate", [NSC, 128, SC], F32, kind=skind).ap()
        dbg_xntok = nc.dram_tensor("dbg_xntok", [NSC, 128, 4 * 1024], BF16, kind=skind).ap()
        dbg_ident = nc.dram_tensor("dbg_ident", [128, 128], BF16, kind=skind).ap()
        dbg_stats = nc.dram_tensor("dbg_stats", [NSC, 128, 128], F32, kind=skind).ap()
    q_sc = nc.dram_tensor("q_sc", [NSC, 128, 8 * SC], BF16, kind=skind).ap()
    k_sc = nc.dram_tensor("k_sc", [NSC, 128, 8 * SC], BF16, kind=skind).ap()
    k_pl = nc.dram_tensor("k_pl", [D, T], BF16, kind=skind).ap()
    v_pl = nc.dram_tensor("v_pl", [T, D], BF16, kind=skind).ap()
    x1_sc = nc.dram_tensor("x1_sc", [T, D], F32, kind=skind).ap()
    xn4_sc = nc.dram_tensor("xn4_sc", [T, D], BF16, kind=skind).ap()
    h_sc = nc.dram_tensor("h_sc", [NT5, 128, 24 * 512], BF16, kind=skind).ap()
    wp_bf = nc.dram_tensor("wp_bf", [D, D], BF16, kind="Internal").ap()
    wg_bf = nc.dram_tensor("wg_bf", [D, HID], BF16, kind="Internal").ap()
    wu_bf = nc.dram_tensor("wu_bf", [D, HID], BF16, kind="Internal").ap()
    wd_bf = nc.dram_tensor("wd_bf", [HID, D], BF16, kind="Internal").ap()

    ctx = ExitStack()
    with ctx:
        P = Prog(nc, ctx)

        def sb(name, shape, dtype):
            return ctx.enter_context(nc.sbuf_tensor(name, shape, dtype))

        def ps(name, shape, dtype=F32):
            return ctx.enter_context(nc.psum_tensor(name, shape, dtype))

        ident = sb("ident", [128, 128], BF16)
        mask_keep = sb("mask_keep", [128, 128], BF16)
        b_qkv = sb("c_b_qkv", [128, 24], F32)
        b_gate = sb("c_b_gate", [128, 8], F32)
        nb_gate = sb("nb_gate", [128, 8], F32)
        b_proj_t = sb("c_b_proj", [128, D], F32)
        b_v_t = sb("c_b_v", [128, D], F32)
        ln_g = sb("c_ln_g", [128, 8], F32)
        ln_b = sb("c_ln_b", [128, 8], F32)
        sq_dump = sb("sq_dump", [128, 1024], F32)
        stats = sb("stats", [128, 128], F32)

        ev_consts = None
        for dst, src in ((b_qkv, b_qkv_d), (b_gate, b_gate_d), (b_proj_t, b_proj_d),
                         (b_v_t, b_v_d), (ln_g, ln_g_d), (ln_b, ln_b_d)):
            ev_consts = P.emit("sync", lambda e, d=dst, s=src: e.dma_start(out=d[:], in_=s),
                               sem="dw", amount=16)
        P.wait("vector", ev_consts)
        P.wait("scalar", ev_consts)
        ev_nbg = P.emit("vector", lambda e: e.tensor_scalar(
            nb_gate[:], b_gate[:], -1.0, None, ALU.mult), sem="dve")
        ev0 = P.emit("gpsimd", lambda e: e.memset(ident[:], 0.0), sem="pool")
        P.wait("gpsimd", ev0)
        ev_ident = P.emit("gpsimd", lambda e: e.affine_select(
            out=ident[:], in_=ident[:], compare_op=ALU.not_equal, fill=1.0,
            base=0, pattern=[[-1, 128]], channel_multiplier=1), sem="pool")
        ev0 = P.emit("gpsimd", lambda e: e.memset(mask_keep[:], 1.0), sem="pool")
        P.wait("gpsimd", ev0)
        ev_mask = P.emit("gpsimd", lambda e: e.affine_select(
            out=mask_keep[:], in_=mask_keep[:], compare_op=ALU.is_ge, fill=0.0,
            base=0, pattern=[[1, 128]], channel_multiplier=-1), sem="pool")

        st_ss = stats[:, 0:16]
        st_sum = stats[:, 16:32]
        st_m = stats[:, 32:48]
        st_v = stats[:, 48:64]
        st_r = stats[:, 64:80]
        st_nmr = stats[:, 80:96]
        st_tmp = stats[:, 96:112]
        st_den = stats[:, 112:116]
        st_rc = stats[:, 116:120]

        def ln_stats(col, x_ap, ev_x, scale_var, bias_var, xtok=None):
            c = col
            xr = (xtok,) if xtok else ()
            ev_s = P.emit("vector", lambda e: e.reduce_sum(
                st_sum[:, c:c + 1], x_ap, axis=AX.X), sem="dve",
                waits=[ev_x], reads=xr, writes=(f"su{c}",))
            ev_q = P.emit("scalar", lambda e: e.activation(
                sq_dump[:], x_ap, AF.Square, accum_out=st_ss[:, c:c + 1]), sem="act",
                waits=[ev_x], reads=xr, writes=("sqd", f"ss{c}"))
            ev_m = P.emit("vector", lambda e: e.tensor_scalar(
                st_m[:, c:c + 1], st_sum[:, c:c + 1], 1.0 / D, None, ALU.mult),
                sem="dve", reads=(f"su{c}",), writes=(f"m{c}",))
            P.emit("vector", lambda e: e.scalar_tensor_tensor(
                st_tmp[:, c:c + 1], st_m[:, c:c + 1], scale_var, st_m[:, c:c + 1],
                ALU.mult, ALU.mult), sem="dve", reads=(f"m{c}",), writes=(f"tp{c}",))
            P.emit("vector", lambda e: e.tensor_scalar(
                st_v[:, c:c + 1], st_ss[:, c:c + 1], scale_var / D, bias_var,
                ALU.mult, ALU.add), sem="dve", reads=(f"ss{c}",), writes=(f"v{c}",))
            ev_v = P.emit("vector", lambda e: e.tensor_sub(
                st_v[:, c:c + 1], st_v[:, c:c + 1], st_tmp[:, c:c + 1]), sem="dve",
                reads=(f"v{c}", f"tp{c}"), writes=(f"v{c}",))
            P.emit("scalar", lambda e: e.activation(
                st_tmp[:, c:c + 1], st_v[:, c:c + 1], AF.Ln), sem="act",
                reads=(f"v{c}",), writes=(f"tp{c}",))
            ev_r = P.emit("scalar", lambda e: e.activation(
                st_r[:, c:c + 1], st_tmp[:, c:c + 1], AF.Exp, scale=-0.5), sem="act",
                reads=(f"tp{c}",), writes=(f"r{c}",))
            ev_n = P.emit("vector", lambda e: e.scalar_tensor_tensor(
                st_nmr[:, c:c + 1], st_m[:, c:c + 1], -1.0, st_r[:, c:c + 1],
                ALU.mult, ALU.mult), sem="dve",
                reads=(f"m{c}", f"r{c}"), writes=(f"n{c}",))
            return st_r[:, c:c + 1], st_nmr[:, c:c + 1], ev_n, ev_r

        if 1 in phases:
            with ExitStack() as p1:
                def sb1(name, shape, dtype):
                    return p1.enter_context(nc.sbuf_tensor(name, shape, dtype))
                w_qkv_b = sb1("w_qkv_b", [128, 8 * 3072], BF16)
                w_gate_b = sb1("w_gate_b", [128, 8 * 1024], BF16)
                xb = [sb1("xb0", [128, CPS * 1024], F32),
                      sb1("xb1", [128, CPS * 1024], F32)]
                xn_tok = sb1("xn_tok", [128, CPS * 1024], BF16)
                xn_fm = sb1("xn_fm", [128, 8 * SC], BF16)
                q_fm = sb1("q_fm", [128, 8 * SC], BF16)
                k_fm = sb1("k_fm", [128, 8 * SC], BF16)
                v_fm = sb1("v_fm", [128, 8 * SC], BF16)
                w_proj_b = sb1("w_proj_b", [128, 8 * 1024], BF16)
                v_tk = sb1("v_tk", [128, CPS * 1024], BF16)
                scr = [sb1(f"scr{i}", [128, SC], F32) for i in range(10)]
                pm = [p1.enter_context(nc.psum_tensor(f"pm{i}", [128, 512], F32))
                      for i in range(4)]
                pT = p1.enter_context(nc.psum_tensor("pT", [128, 2048], BF16))

                ev_w1 = None
                for kk in range(8):
                    ev_w1 = P.emit("gpsimd", lambda e, kk=kk: e.dma_start(
                        out=w_qkv_b[:, kk * 3072:(kk + 1) * 3072],
                        in_=w_qkv_d[kk * 128:(kk + 1) * 128, :]), sem="dwp", amount=16)
                for kk in range(8):
                    ev_w1 = P.emit("gpsimd", lambda e, kk=kk: e.dma_start(
                        out=w_gate_b[:, kk * 1024:(kk + 1) * 1024],
                        in_=w_gate_d[kk * 128:(kk + 1) * 128, :]), sem="dwp", amount=16)
                ev_cast_wp = P.emit("gpsimd", lambda e: e.dma_start(
                    out=wp_bf, in_=w_proj_d), sem="cwp", amount=16)
                P.wait("sync", ev_cast_wp)
                ev_wp = None
                for kk in range(8):
                    ev_wp = P.emit("sync", lambda e, kk=kk: e.dma_start(
                        out=w_proj_b[:, kk * 1024:(kk + 1) * 1024],
                        in_=wp_bf[kk * 128:(kk + 1) * 128, :]), sem="lwp", amount=16)
                ev_cast_wg = P.emit("gpsimd", lambda e: e.dma_start(
                    out=wg_bf, in_=wg_d), sem="cwg", amount=16)
                ev_cast_wu = P.emit("gpsimd", lambda e: e.dma_start(
                    out=wu_bf, in_=wu_d), sem="cwu", amount=16)
                ev_cast_wd = P.emit("gpsimd", lambda e: e.dma_start(
                    out=wd_bf, in_=wd_d), sem="cwd", amount=16)

                ev_x = [None, None]
                ev_xb_free = [None, None]
                ev_pm_free = [None, None, None, None]
                ev_pT_free = [None, None]
                ev_mm_last = None
                ev_store = {"q": None, "k": None, "kp": None, "v": None}

                def load_x(g):
                    par = g % 2
                    evs = []
                    if ev_xb_free[par] is not None:
                        P.waits("sync", ev_xb_free[par])
                    for c in range(CPS):
                        evs.append(P.emit("sync", lambda e, g=g, c=c, par=par:
                                          e.dma_start(out=xb[par][:, c * 1024:(c + 1) * 1024],
                                                      in_=x_d[g * SC + c * 128:
                                                              g * SC + (c + 1) * 128, :]),
                                          sem="da" if par == 0 else "db", amount=16))
                    ev_x[par] = evs[-1]

                def emit_ln12(g):
                    par = g % 2
                    xcur = xb[par]
                    sv = 1.0 + EPS
                    bv = EPS * EPS
                    ev_x_g = ev_x[par]
                    xc = [xcur[:, c * 1024:(c + 1) * 1024] for c in range(CPS)]
                    for c in range(CPS):
                        P.emit("vector", lambda e, c=c: e.reduce_sum(
                            st_sum[:, c:c + 1], xc[c], axis=AX.X), sem="dve",
                            waits=[ev_x_g], writes=(f"su{c}",))
                    for c in range(CPS):
                        P.emit("scalar", lambda e, c=c: e.activation(
                            sq_dump[:], xc[c], AF.Square,
                            accum_out=st_ss[:, c:c + 1]), sem="act",
                            waits=[ev_x_g], writes=("sqd", f"ss{c}"))
                    for c in range(CPS):
                        P.emit("vector", lambda e, c=c: e.tensor_scalar(
                            st_m[:, c:c + 1], st_sum[:, c:c + 1], 1.0 / D, None,
                            ALU.mult), sem="dve", reads=(f"su{c}",),
                            writes=(f"m{c}",))
                    for c in range(CPS):
                        P.emit("vector", lambda e, c=c: e.scalar_tensor_tensor(
                            st_tmp[:, c:c + 1], st_m[:, c:c + 1], sv,
                            st_m[:, c:c + 1], ALU.mult, ALU.mult), sem="dve",
                            reads=(f"m{c}",), writes=(f"tp{c}",))
                    for c in range(CPS):
                        P.emit("vector", lambda e, c=c: e.tensor_scalar(
                            st_v[:, c:c + 1], st_ss[:, c:c + 1], sv / D, bv,
                            ALU.mult, ALU.add), sem="dve", reads=(f"ss{c}",),
                            writes=(f"v{c}",))
                    for c in range(CPS):
                        P.emit("vector", lambda e, c=c: e.tensor_sub(
                            st_v[:, c:c + 1], st_v[:, c:c + 1],
                            st_tmp[:, c:c + 1]), sem="dve",
                            reads=(f"v{c}", f"tp{c}"), writes=(f"v{c}",))
                    for c in range(CPS):
                        P.emit("scalar", lambda e, c=c: e.activation(
                            st_tmp[:, c:c + 1], st_v[:, c:c + 1], AF.Ln),
                            sem="act", reads=(f"v{c}",), writes=(f"tp{c}",))
                    for c in range(CPS):
                        P.emit("scalar", lambda e, c=c: e.activation(
                            st_r[:, c:c + 1], st_tmp[:, c:c + 1], AF.Exp,
                            scale=-0.5), sem="act", reads=(f"tp{c}",),
                            writes=(f"r{c}",))
                    ev_n = None
                    for c in range(CPS):
                        ev_n = P.emit("vector", lambda e, c=c: e.scalar_tensor_tensor(
                            st_nmr[:, c:c + 1], st_m[:, c:c + 1], -1.0,
                            st_r[:, c:c + 1], ALU.mult, ALU.mult), sem="dve",
                            reads=(f"m{c}", f"r{c}"), writes=(f"n{c}",))
                    evn = []
                    for c in range(CPS):
                        ev = P.emit("vector", lambda e, c=c: e.tensor_scalar(
                            xn_tok[:, c * 1024:(c + 1) * 1024],
                            xc[c], st_r[:, c:c + 1], st_nmr[:, c:c + 1],
                            ALU.mult, ALU.add), sem="dve",
                            waits=[ev_pT_free[0], ev_pT_free[1]],
                            reads=(f"r{c}", f"n{c}"))
                        evn.append(ev)
                    ev_xb_free[par] = [evn[-1], ev_n]
                    return evn

                load_x(0)
                evn = emit_ln12(0)
                for g in range(NSC):
                    par = g % 2
                    if g + 1 < NSC:
                        load_x(g + 1)
                    ev_xn = []
                    for j in range(8):
                        tpar = j % 2
                        P.wait("tensor", evn[-1])
                        P.wait("tensor", ev_ident)
                        P.wait("tensor", ev_pT_free[tpar])
                        P.wait("tensor", ev_mm_last)
                        evt = None
                        for c in range(CPS):
                            evt = P.emit("tensor", lambda e, c=c, j=j, tpar=tpar:
                                         e.transpose(pT[:, tpar * 1024 + c * 128:
                                                        tpar * 1024 + (c + 1) * 128],
                                                     xn_tok[:, c * 1024 + j * 128:
                                                            c * 1024 + (j + 1) * 128],
                                                     ident[:]), sem="mm")
                        P.wait("scalar", evt)
                        ev = P.emit("scalar", lambda e, j=j, tpar=tpar:
                                    e.activation(xn_fm[:, j * SC:(j + 1) * SC],
                                                 pT[:, tpar * 1024:tpar * 1024 + 512],
                                                 AF.Identity, bias=ln_b[:, j:j + 1],
                                                 scale=ln_g[:, j:j + 1]), sem="act")
                        ev_pT_free[tpar] = ev
                        ev_xn.append(ev)
                    slot_rr = [0]

                    def mm8(wsb, col0, extra_waits=()):
                        psl = slot_rr[0]
                        slot_rr[0] = (psl + 1) % 4
                        P.wait("tensor", ev_xn[-1])
                        P.wait("tensor", ev_w1)
                        P.wait("tensor", ev_pm_free[psl])
                        for w in extra_waits:
                            P.wait("tensor", w)
                        ev = None
                        for kk in range(8):
                            ev = P.emit("tensor", lambda e, psl=psl, wsb=wsb, kk=kk, col0=col0:
                                        e.matmul(pm[psl][:],
                                                 wsb[:, kk * (wsb.shape[1] // 8) + col0:
                                                     kk * (wsb.shape[1] // 8) + col0 + 128],
                                                 xn_fm[:, kk * SC:(kk + 1) * SC],
                                                 start=(kk == 0), stop=(kk == 7)),
                                        sem="mm")
                        return psl, ev

                    pend_q = []

                    def emit_gate_k(j):
                        sA, sB, sC, sD, sE = (scr[(j % 2) * 5 + i] for i in range(5))
                        tA, tB, tC, tD, tE = (f"scr{(j % 2) * 5 + i}" for i in range(5))
                        gsl, evg = mm8(w_gate_b, j * 128)
                        ev_e1 = P.emit("scalar", lambda e, sA=sA, j=j, gsl=gsl:
                                       e.activation(sA[:], pm[gsl][:], AF.Exp,
                                                    scale=-1.0,
                                                    bias=nb_gate[:, j:j + 1]),
                                       sem="act", waits=[evg, ev_nbg], writes=(tA,))
                        ev_pm_free[gsl] = ev_e1
                        P.emit("scalar", lambda e, sA=sA, sB=sB:
                               e.activation(sB[:], sA[:], AF.Ln, bias=1.0),
                               sem="act", reads=(tA,), writes=(tB,))
                        ev_gate = P.emit("scalar", lambda e, sB=sB, sA=sA: e.activation(
                            sA[:], sB[:], AF.Exp, scale=-1.0), sem="act",
                            reads=(tB,), writes=(tA,))
                        ksl, evk = mm8(w_qkv_b, 1024 + j * 128)
                        ev_stt = P.emit("vector", lambda e, sC=sC, j=j, sA=sA, ksl=ksl:
                                        e.scalar_tensor_tensor(
                                            sC[:], pm[ksl][:], b_qkv[:, 8 + j:9 + j],
                                            sA[:], ALU.add, ALU.mult), sem="dve",
                                        waits=[evk], reads=(tA,), writes=(tC,))
                        ev_pm_free[ksl] = ev_stt
                        P.emit("vector", lambda e, sC=sC, sD=sD:
                               e.tensor_scalar_max(sD[:], sC[:], 0.0), sem="dve",
                               reads=(tC,), writes=(tD,))
                        P.emit("vector", lambda e, sC=sC: e.tensor_scalar_min(
                            sC[:], sC[:], 0.0), sem="dve", reads=(tC,), writes=(tC,))
                        P.emit("scalar", lambda e, sC=sC, sE=sE: e.activation(
                            sE[:], sC[:], AF.Exp), sem="act",
                            reads=(tC,), writes=(tE,))
                        ev_kj = P.emit("vector", lambda e, sD=sD, sE=sE, j=j:
                                       e.tensor_add(k_fm[:, j * SC:(j + 1) * SC],
                                                    sD[:], sE[:]), sem="dve",
                                       waits=[ev_store["k"], ev_store["kp"]],
                                       reads=(tD, tE))
                        return ev_kj

                    def emit_q(j):
                        sA, sB, sC, sD, sE = (scr[(j % 2) * 5 + i] for i in range(5))
                        tA, tB, tC, tD, tE = (f"scr{(j % 2) * 5 + i}" for i in range(5))
                        qsl, evq = mm8(w_qkv_b, j * 128)
                        P.emit("vector", lambda e, sD=sD, j=j, qsl=qsl: e.tensor_scalar(
                            sD[:], pm[qsl][:], b_qkv[:, j:j + 1], 0.0, ALU.add,
                            ALU.max), sem="dve", waits=[evq], writes=(tD,))
                        ev_pm_free[qsl] = P.emit(
                            "vector", lambda e, sC=sC, j=j, qsl=qsl:
                            e.tensor_scalar(sC[:], pm[qsl][:], b_qkv[:, j:j + 1],
                                            0.0, ALU.add, ALU.min), sem="dve",
                            writes=(tC,))
                        P.emit("scalar", lambda e, sC=sC, sE=sE: e.activation(
                            sE[:], sC[:], AF.Exp), sem="act",
                            reads=(tC,), writes=(tE,))
                        ev_qj = P.emit("vector", lambda e, sD=sD, sE=sE, j=j:
                                       e.tensor_add(q_fm[:, j * SC:(j + 1) * SC],
                                                    sD[:], sE[:]), sem="dve",
                                       waits=[ev_store["q"]], reads=(tD, tE))
                        return ev_qj

                    ev_kj = ev_qj = None
                    for j in range(8):
                        ev_kj = emit_gate_k(j)
                        if j >= 1:
                            ev_qj = emit_q(j - 1)
                        if j == 5 and g + 1 < NSC:
                            evn = emit_ln12(g + 1)
                        ev_gate_dbg = None
                    ev_qj = emit_q(7)
                    ev_vv = []
                    ev_vfm = []
                    for j in range(8):
                        vsl, evv0 = mm8(w_qkv_b, 2048 + j * 128)
                        ev = P.emit("scalar", lambda e, j=j, vsl=vsl: e.activation(
                            v_fm[:, j * SC:(j + 1) * SC], pm[vsl][:], AF.Identity,
                            bias=b_qkv[:, 16 + j:17 + j]), sem="act",
                            waits=[evv0])
                        ev_pm_free[vsl] = ev
                        ev_vfm.append(ev)
                    for c in range(CPS):
                        s0 = slot_rr[0]; slot_rr[0] = (s0 + 1) % 4
                        s1 = slot_rr[0]; slot_rr[0] = (s1 + 1) % 4
                        P.waits("tensor", [ev_vfm[-1], ev_wp, ev_pm_free[s0],
                                           ev_pm_free[s1]])
                        ev0 = ev1 = None
                        for kk in range(8):
                            ev0 = P.emit("tensor", lambda e, c=c, kk=kk, s0=s0: e.matmul(
                                pm[s0][:],
                                v_fm[:, kk * SC + c * 128:kk * SC + (c + 1) * 128],
                                w_proj_b[:, kk * 1024:kk * 1024 + 512],
                                start=(kk == 0), stop=(kk == 7)),
                                sem="mm" if kk == 7 else None)
                            ev1 = P.emit("tensor", lambda e, c=c, kk=kk, s1=s1: e.matmul(
                                pm[s1][:],
                                v_fm[:, kk * SC + c * 128:kk * SC + (c + 1) * 128],
                                w_proj_b[:, kk * 1024 + 512:(kk + 1) * 1024],
                                start=(kk == 0), stop=(kk == 7)),
                                sem="mm" if kk == 7 else None)
                        P.wait("vector", ev1)
                        P.wait("vector", ev_store["v"])
                        ev_pm_free[s0] = P.emit("vector", lambda e, c=c, s0=s0:
                                                e.tensor_copy(
                                                    v_tk[:, c * 1024:c * 1024 + 512],
                                                    pm[s0][:]), sem="dve")
                        ev_pm_free[s1] = P.emit("vector", lambda e, c=c, s1=s1:
                                                e.tensor_copy(
                                                    v_tk[:, c * 1024 + 512:
                                                         (c + 1) * 1024],
                                                    pm[s1][:]), sem="dve")
                        ev_vv.append(ev_pm_free[s1])
                    ev_mm_last = ev1
                    if dev_scratch:
                        P.waits("sync", [ev_xn[-1], ev_gate_dbg])
                        P.emit("sync", lambda e, g=g: e.dma_start(
                            out=dbg_xn[g], in_=xn_fm[:]), sem="so", amount=16)
                        P.emit("sync", lambda e, g=g: e.dma_start(
                            out=dbg_gate[g], in_=scr[5][:]), sem="so", amount=16)
                        P.emit("sync", lambda e, g=g: e.dma_start(
                            out=dbg_xntok[g], in_=xn_tok[:]), sem="so", amount=16)
                        P.emit("sync", lambda e, g=g: e.dma_start(
                            out=dbg_stats[g], in_=stats[:]), sem="so", amount=16)
                        if g == 0:
                            P.emit("sync", lambda e: e.dma_start(
                                out=dbg_ident, in_=ident[:]), sem="so", amount=16)
                    ev_q_all = ev_qj
                    ev_k_all = ev_kj
                    P.wait("sync", ev_q_all)
                    ev_store["q"] = P.emit("sync", lambda e, g=g: e.dma_start(
                        out=q_sc[g], in_=q_fm[:]), sem="sq", amount=16)
                    P.wait("sync", ev_k_all)
                    ev_store["k"] = P.emit("sync", lambda e, g=g: e.dma_start(
                        out=k_sc[g], in_=k_fm[:]), sem="sk", amount=16)
                    evkp = None
                    for j in range(8):
                        evkp = P.emit("sync", lambda e, g=g, j=j: e.dma_start(
                            out=k_pl[j * 128:(j + 1) * 128, g * SC:(g + 1) * SC],
                            in_=k_fm[:, j * SC:(j + 1) * SC]), sem="skp", amount=16)
                    ev_store["kp"] = evkp
                    evv = None
                    for c in range(CPS):
                        P.wait("sync", ev_vv[c])
                        evv = P.emit("sync", lambda e, g=g, c=c: e.dma_start(
                            out=v_pl[g * SC + c * 128:g * SC + (c + 1) * 128, :],
                            in_=v_tk[:, c * 1024:(c + 1) * 1024]), sem="sv", amount=16)
                    ev_store["v"] = evv
                p1_done = P.snapshot()
            P.run_block()

        if 2 in phases:
            with ExitStack() as p2:
                def sb2(name, shape, dtype):
                    return p2.enter_context(nc.sbuf_tensor(name, shape, dtype))
                S_f = sb2("S_f", [128, 8 * 1025], F32)
                S_b = sb2("S_b", [128, 8 * 1025], BF16)
                q_sb = [sb2(f"q_sb{i}", [128, 8 * SC], BF16) for i in range(2)]
                k_sb = [sb2(f"k_sb{i}", [128, 8 * SC], BF16) for i in range(2)]
                kT_sb = [sb2(f"kT_sb{i}", [128, CPS * 1024], BF16) for i in range(2)]
                v_sb = [sb2(f"v_sb{i}", [128, CPS * 1025], BF16) for i in range(2)]
                x_sb = [sb2(f"x_sb{i}", [128, CPS * 1024], F32) for i in range(2)]
                aT_sb = sb2("aT_sb", [128, 2 * CPS * 128], BF16)
                ao_sb = sb2("ao_sb", [128, 1024], F32)
                t3_sb = sb2("t3_sb", [128, 1024], F32)
                xn4_tok = [sb2(f"xn4_{i}", [128, 1024], BF16) for i in range(2)]
                pA = p2.enter_context(nc.psum_tensor("pA", [128, 1280], F32))
                pO = p2.enter_context(nc.psum_tensor("pO", [128, 1280], F32))

                if 1 in phases:
                    P.barrier(p1_done)
                ev_Sf0 = P.emit("gpsimd", lambda e: e.memset(S_f[:], 0.0), sem="pool")
                ev_S0 = P.emit("gpsimd", lambda e: e.memset(S_b[:], 0.0), sem="pool")
                ev_vones = None
                for par in range(2):
                    for c in range(CPS):
                        ev_vones = P.emit("gpsimd", lambda e, par=par, c=c: e.memset(
                            v_sb[par][:, c * 1025 + 1024:(c + 1) * 1025], 1.0), sem="pool")

                ev_ld = [None, None]
                ev_ldT = [None, None]
                ev_buf_free = [None, None]
                ev_S_cast = ev_S0
                ev_S_read = None
                ev_pA_free = [None, None]
                ev_pO_free = None
                ev_x1st = None
                ev_xn4st = [None, None]
                ev_aT_free = [[None] * CPS, [None] * CPS]
                ev_pairs = {}

                def load_sc(g):
                    par = g % 2
                    if ev_buf_free[par] is not None:
                        P.waits("sync", ev_buf_free[par])
                    evs = []
                    dsem = "da" if par == 0 else "db"
                    evs.append(P.emit("sync", lambda e, g=g, par=par: e.dma_start(
                        out=q_sb[par][:], in_=q_sc[g]), sem=dsem, amount=16))
                    evs.append(P.emit("sync", lambda e, g=g, par=par: e.dma_start(
                        out=k_sb[par][:], in_=k_sc[g]), sem=dsem, amount=16))
                    for c in range(CPS):
                        evs.append(P.emit("sync", lambda e, g=g, c=c, par=par: e.dma_start(
                            out=v_sb[par][:, c * 1025:c * 1025 + 1024],
                            in_=v_pl[g * SC + c * 128:g * SC + (c + 1) * 128, :]),
                            sem=dsem, amount=16))
                        evs.append(P.emit("sync", lambda e, g=g, c=c, par=par: e.dma_start(
                            out=x_sb[par][:, c * 1024:(c + 1) * 1024],
                            in_=x_d[g * SC + c * 128:g * SC + (c + 1) * 128, :]),
                            sem=dsem, amount=16))
                    ev_ld[par] = evs[-1]
                    if ev_buf_free[par] is not None:
                        P.waits("scalar", ev_buf_free[par])
                    evT = None
                    for c in range(CPS):
                        evT = P.emit("scalar", lambda e, g=g, c=c, par=par: e.dma_start(
                            out=kT_sb[par][:, c * 1024:(c + 1) * 1024],
                            in_=k_pl[:, (g * CPS + c) * 128:(g * CPS + c + 1) * 128],
                            transpose=True), sem="dta" if par == 0 else "dtb",
                            amount=16)
                    ev_ldT[par] = evT

                def emit_pairs(t):
                    g, c = t // CPS, t % CPS
                    par = g % 2
                    pp = (t % 2) * CPS * 128
                    ks, qs = k_sb[par], q_sb[par]
                    out = []
                    for cp in range(c + 1):
                        sl = cp % 2
                        off = 0 if sl == 0 else 512
                        P.waits("tensor", [ev_ld[par], ev_pA_free[sl]])
                        evt = None
                        for kk in range(8):
                            evt = P.emit("tensor",
                                         lambda e, kk=kk, cp=cp, c=c, off=off,
                                         ks=ks, qs=qs:
                                         e.matmul(pA[:, off:off + 128],
                                                  ks[:, kk * SC + cp * 128:
                                                     kk * SC + (cp + 1) * 128],
                                                  qs[:, kk * SC + c * 128:
                                                     kk * SC + (c + 1) * 128],
                                                  start=(kk == 0), stop=(kk == 7)),
                                         sem="mm" if kk == 7 else None)
                        if cp == c:
                            ev = P.emit("vector", lambda e, cp=cp, off=off, pp=pp:
                                        e.tensor_mul(
                                            aT_sb[:, pp + cp * 128:pp + (cp + 1) * 128],
                                            pA[:, off:off + 128],
                                            mask_keep[:]), sem="dve",
                                        waits=[evt, ev_mask,
                                               ev_aT_free[t % 2][cp]])
                        else:
                            ev = P.emit("scalar", lambda e, cp=cp, off=off, pp=pp:
                                        e.activation(
                                            aT_sb[:, pp + cp * 128:pp + (cp + 1) * 128],
                                            pA[:, off:off + 128],
                                            AF.Identity), sem="act",
                                        waits=[evt, ev_aT_free[t % 2][cp]])
                        ev_pA_free[sl] = ev
                        out.append(ev)
                    return out

                load_sc(0)
                ev_pairs[0] = emit_pairs(0)
                for g in range(NSC):
                    par = g % 2
                    if g + 1 < NSC:
                        load_sc(g + 1)
                    qs, ks, kTs, vs, xs = (q_sb[par], k_sb[par], kT_sb[par],
                                           v_sb[par], x_sb[par])
                    for c in range(CPS):
                        tglob = g * CPS + c
                        pp = (tglob % 2) * CPS * 128
                        evacs = ev_pairs.pop(tglob)
                        n_per = {0: (c + 1) + (8 if g > 0 else 0)}
                        idx = [0]
                        P.waits("tensor", [ev_pO_free, ev_ld[par], ev_vones])
                        ev_av_last = None
                        for cp in range(c + 1):
                            P.wait("tensor", evacs[cp])
                            for (o0, w) in ((0, 512), (512, 512), (1024, 1)):
                                st = idx[0] == 0
                                sp = (g == 0 and cp == c)
                                if o0 == 1024:
                                    idx[0] += 1
                                ev_av_last = P.emit(
                                    "tensor",
                                    lambda e, cp=cp, o0=o0, w=w, st=st, sp=sp,
                                    vs=vs, pp=pp:
                                    e.matmul(
                                        pO[:, o0:o0 + w],
                                        aT_sb[:, pp + cp * 128:pp + (cp + 1) * 128],
                                        vs[:, cp * 1025 + o0:cp * 1025 + o0 + w],
                                        start=st, stop=sp),
                                    sem="mm" if (cp == c and o0 == 1024) else None)
                        ev_cross_last = ev_av_last
                        if g > 0:
                            P.waits("tensor", [ev_S_cast])
                            for j in range(8):
                                for (o0, w) in ((0, 512), (512, 512), (1024, 1)):
                                    ev_cross_last = P.emit(
                                        "tensor",
                                        lambda e, j=j, c=c, o0=o0, w=w, qs=qs:
                                        e.matmul(
                                            pO[:, o0:o0 + w],
                                            qs[:, j * SC + c * 128:j * SC + (c + 1) * 128],
                                            S_b[:, j * 1025 + o0:j * 1025 + o0 + w],
                                            start=False, stop=(j == 7)),
                                        sem="mm" if (j == 7 and o0 == 1024) else None)
                            ev_S_read = ev_cross_last
                        ev_av_last = ev_cross_last
                        for cp in range(c + 1):
                            ev_aT_free[tglob % 2][cp] = ev_av_last
                        if tglob + 1 < NCH:
                            ev_pairs[tglob + 1] = emit_pairs(tglob + 1)
                        cst = c % 4
                        ev_d1 = P.emit("vector", lambda e, cst=cst: e.tensor_scalar(
                            st_den[:, cst:cst + 1], pO[:, 1024:1025], DEN_EPS, None,
                            ALU.add), sem="dve", waits=[ev_av_last],
                            writes=(f"dn{cst}",))
                        ev_d2 = P.emit("vector", lambda e, cst=cst: e.reciprocal(
                            st_rc[:, cst:cst + 1], st_den[:, cst:cst + 1]), sem="dve",
                            reads=(f"dn{cst}",), writes=(f"rc{cst}",))
                        tp = c % 2
                        ev_ao = P.emit("vector", lambda e, cst=cst:
                                       e.scalar_tensor_tensor(
                                           ao_sb[:], pO[:, 0:1024],
                                           st_rc[:, cst:cst + 1], b_proj_t[:],
                                           ALU.mult, ALU.add), sem="dve",
                                       waits=[ev_av_last],
                                       reads=(f"rc{cst}",), writes=("ao",))
                        ev_pO_free = ev_ao
                        r3, n3, evn3, _ = ln_stats(4 + cst, ao_sb[:], ev_ao, 1.0, EPS,
                                                   xtok="ao")
                        x1ap = xs[:, c * 1024:(c + 1) * 1024]
                        ev_t3 = P.emit("vector", lambda e, r3=r3, n3=n3: e.tensor_scalar(
                            t3_sb[:], ao_sb[:], r3, n3, ALU.mult, ALU.add), sem="dve",
                            waits=[evn3],
                            reads=("ao", f"r{4 + cst}", f"n{4 + cst}"), writes=("t3",))
                        ev_x1 = P.emit("vector", lambda e, x1ap=x1ap: e.tensor_add(
                            x1ap, x1ap, t3_sb[:]), sem="dve",
                            waits=[ev_ld[par], ev_x1st], reads=("t3",))
                        r4, n4, evn4, ev_r4 = ln_stats(8 + cst, x1ap, ev_x1, 1.0, EPS)
                        ev_xn4 = P.emit("scalar", lambda e, x1ap=x1ap, r4=r4, n4=n4, tp=tp:
                                        e.activation(xn4_tok[tp][:], x1ap, AF.Identity,
                                                     bias=n4, scale=r4), sem="act",
                                        waits=[evn4, ev_xn4st[tp]],
                                        reads=(f"r{8 + cst}", f"n{8 + cst}"))
                        P.wait("sync", ev_x1)
                        ev_x1st = P.emit("sync", lambda e, tglob=tglob, x1ap=x1ap:
                                         e.dma_start(out=x1_sc[tglob * 128:
                                                              (tglob + 1) * 128, :],
                                                     in_=x1ap), sem="sx1", amount=16)
                        P.wait("sync", ev_xn4)
                        ev_xn4st[tp] = P.emit("sync", lambda e, tglob=tglob, tp=tp:
                                              e.dma_start(out=xn4_sc[tglob * 128:
                                                                     (tglob + 1) * 128, :],
                                                          in_=xn4_tok[tp][:]),
                                              sem="sxn0" if tp == 0 else "sxn1",
                                              amount=16)
                    ev_ds_ev = None
                    for j in range(8):
                        if j % 2 == 0:
                            reg = pO
                            P.waits("tensor", [ev_pO_free])
                        else:
                            reg = pA
                            P.waits("tensor", [ev_pA_free[0], ev_pA_free[1]])
                        P.waits("tensor", [ev_ldT[par], ev_ld[par], ev_vones])
                        evm = None
                        for c in range(CPS):
                            for (o0, w) in ((0, 512), (512, 512), (1024, 1)):
                                evm = P.emit("tensor",
                                             lambda e, j=j, c=c, o0=o0, w=w, reg=reg,
                                             kTs=kTs, vs=vs:
                                             e.matmul(reg[:, o0:o0 + w],
                                                      kTs[:, c * 1024 + j * 128:
                                                          c * 1024 + (j + 1) * 128],
                                                      vs[:, c * 1025 + o0:
                                                         c * 1025 + o0 + w],
                                                      start=(c == 0), stop=(c == CPS - 1)),
                                             sem="mm")
                        ev_dve = P.emit("vector", lambda e, j=j, reg=reg: e.tensor_add(
                            S_f[:, j * 1025:(j + 1) * 1025],
                            S_f[:, j * 1025:(j + 1) * 1025],
                            reg[:, 0:1025]), sem="dve",
                            waits=[evm, ev_S_read, ev_Sf0],
                            reads=(f"Sf{j}",), writes=(f"Sf{j}",))
                        if j % 2 == 0:
                            ev_pO_free = ev_dve
                        else:
                            ev_pA_free[0] = ev_dve
                            ev_pA_free[1] = ev_dve
                        ev_ds_ev = P.emit("gpsimd", lambda e, j=j: e.tensor_copy(
                            S_b[:, j * 1025:(j + 1) * 1025],
                            S_f[:, j * 1025:(j + 1) * 1025]), sem="pool",
                            reads=(f"Sf{j}",))
                    ev_S_cast = ev_ds_ev
                    ev_buf_free[par] = [evm, ev_x1st]
                p2_done = P.snapshot()
            P.run_block()

        if 4 in phases:
            with ExitStack() as p4:
                def sb4(name, shape, dtype):
                    return p4.enter_context(nc.sbuf_tensor(name, shape, dtype))
                wg_b = sb4("wg_b", [128, 8 * 3072], BF16)
                wu_b = sb4("wu_b", [128, 8 * 3072], BF16)
                xn4_fm = [sb4(f"xn4fm{i}", [128, 8 * 512], BF16) for i in range(2)]
                h_sb = [sb4(f"h_sb{i}", [128, 24 * 512], BF16) for i in range(2)]
                silu_s = [sb4(f"silu{i}", [128, 512], F32) for i in range(2)]
                pg = [p4.enter_context(nc.psum_tensor(f"pg{i}", [128, 512], F32))
                      for i in range(2)]
                pu = [p4.enter_context(nc.psum_tensor(f"pu{i}", [128, 512], F32))
                      for i in range(2)]

                if 2 in phases:
                    P.barrier(p2_done)
                P.wait("sync", ev_cast_wg)
                P.wait("sync", ev_cast_wu)
                ev_wg4 = None
                for kk in range(8):
                    ev_wg4 = P.emit("sync", lambda e, kk=kk: e.dma_start(
                        out=wg_b[:, kk * 3072:(kk + 1) * 3072],
                        in_=wg_bf[kk * 128:(kk + 1) * 128, :]), sem="lwg", amount=16)
                P.wait("gpsimd", ev_cast_wu)
                ev_w4 = None
                for kk in range(8):
                    ev_w4 = P.emit("gpsimd", lambda e, kk=kk: e.dma_start(
                        out=wu_b[:, kk * 3072:(kk + 1) * 3072],
                        in_=wu_bf[kk * 128:(kk + 1) * 128, :]), sem="lwu", amount=16)

                ev_xnl = [None, None]
                ev_xn_free = [None, None]
                ev_pg_free = [None, None]
                ev_pu_free = [None, None]
                ev_silu_free = [None, None]
                ev_hst = [None, None]

                def load_xn4(t):
                    par = t % 2
                    P.wait("scalar", ev_xn_free[par])
                    ev = None
                    for j in range(8):
                        ev = P.emit("scalar", lambda e, t=t, j=j, par=par: e.dma_start(
                            out=xn4_fm[par][:, j * 512:(j + 1) * 512],
                            in_=xn4_sc[t * 512:(t + 1) * 512, j * 128:(j + 1) * 128],
                            transpose=True), sem="dta" if par == 0 else "dtb",
                            amount=16)
                    ev_xnl[par] = ev

                load_xn4(0)
                for t in range(NT5):
                    par = t % 2
                    if t + 1 < NT5:
                        load_xn4(t + 1)
                    ev_mm_xn = None
                    ev_h_last = None
                    for hc in range(24):
                        sl = hc % 2
                        P.waits("tensor", [ev_xnl[par], ev_wg4, ev_pg_free[sl],
                                           ev_pu_free[sl]])
                        evg = evu = None
                        for kk in range(8):
                            evg = P.emit("tensor", lambda e, hc=hc, kk=kk, sl=sl, par=par:
                                         e.matmul(pg[sl][:],
                                                  wg_b[:, kk * 3072 + hc * 128:
                                                       kk * 3072 + (hc + 1) * 128],
                                                  xn4_fm[par][:, kk * 512:(kk + 1) * 512],
                                                  start=(kk == 0), stop=(kk == 7)),
                                         sem="mm")
                        P.wait("tensor", ev_w4)
                        for kk in range(8):
                            evu = P.emit("tensor", lambda e, hc=hc, kk=kk, sl=sl, par=par:
                                         e.matmul(pu[sl][:],
                                                  wu_b[:, kk * 3072 + hc * 128:
                                                       kk * 3072 + (hc + 1) * 128],
                                                  xn4_fm[par][:, kk * 512:(kk + 1) * 512],
                                                  start=(kk == 0), stop=(kk == 7)),
                                         sem="mm")
                        ev_mm_xn = evu
                        P.waits("scalar", [evg, ev_silu_free[sl]])
                        ev_sl = P.emit("scalar", lambda e, sl=sl: e.activation(
                            silu_s[sl][:], pg[sl][:], AF.Silu, scale=QSCALE), sem="act")
                        ev_pg_free[sl] = ev_sl
                        P.waits("vector", [ev_sl, evu, ev_hst[par]])
                        ev_h = P.emit("vector", lambda e, hc=hc, sl=sl, par=par:
                                      e.scalar_tensor_tensor(
                                          h_sb[par][:, hc * 512:(hc + 1) * 512],
                                          pu[sl][:], QSCALE, silu_s[sl][:],
                                          ALU.mult, ALU.mult), sem="dve")
                        ev_pu_free[sl] = ev_h
                        ev_silu_free[sl] = ev_h
                        ev_h_last = ev_h
                    ev_xn_free[par] = ev_mm_xn
                    P.wait("sync", ev_h_last)
                    ev_hst[par] = P.emit("sync", lambda e, t=t, par=par: e.dma_start(
                        out=h_sc[t], in_=h_sb[par][:]),
                        sem="sh0" if par == 0 else "sh1", amount=16)
                p4_done = P.snapshot()
            P.run_block()

        if 5 in phases:
            with ExitStack() as p5:
                def sb5(name, shape, dtype):
                    return p5.enter_context(nc.sbuf_tensor(name, shape, dtype))
                wd_b = sb5("wd_b", [128, 24 * 1024], BF16)
                h5 = [sb5(f"h5_{i}", [128, 24 * 512], BF16) for i in range(2)]
                x15 = [sb5(f"x15_{i}", [128, 4 * 1024], F32) for i in range(2)]
                y_sb = sb5("y_sb", [128, 1024], F32)
                t5_sb = sb5("t5_sb", [128, 1024], F32)
                py = [p5.enter_context(nc.psum_tensor(f"py{i}", [128, 1024], F32))
                      for i in range(2)]

                if 4 in phases:
                    P.barrier(p4_done)
                ev_w5 = None
                P.wait("sync", ev_cast_wd)
                P.wait("gpsimd", ev_cast_wd)
                for kk in range(24):
                    eng5 = "sync" if kk % 2 == 0 else "gpsimd"
                    sem5 = "lwd" if kk % 2 == 0 else "dwp"
                    ev_w5a = P.emit(eng5, lambda e, kk=kk: e.dma_start(
                        out=wd_b[:, kk * 1024:(kk + 1) * 1024],
                        in_=wd_bf[kk * 128:(kk + 1) * 128, :]), sem=sem5, amount=16)
                    if kk % 2 == 0:
                        ev_w5 = ev_w5a
                    else:
                        ev_w5b = ev_w5a

                ev_l5 = [None, None]
                ev_b5_free = [None, None]
                ev_py_free = [None, None]
                ev_y_free = None
                ev_ost = None

                def load5(t):
                    par = t % 2
                    if ev_b5_free[par] is not None:
                        P.waits("sync", ev_b5_free[par])
                    dsem = "da" if par == 0 else "db"
                    P.emit("sync", lambda e, t=t, par=par: e.dma_start(
                        out=h5[par][:], in_=h_sc[t]), sem=dsem, amount=16)
                    evs = None
                    for c in range(4):
                        evs = P.emit("sync", lambda e, t=t, c=c, par=par: e.dma_start(
                            out=x15[par][:, c * 1024:(c + 1) * 1024],
                            in_=x1_sc[t * 512 + c * 128:t * 512 + (c + 1) * 128, :]),
                            sem=dsem, amount=16)
                    ev_l5[par] = evs

                load5(0)
                for t in range(NT5):
                    par = t % 2
                    if t + 1 < NT5:
                        load5(t + 1)
                    ev_mm5 = None
                    for c in range(4):
                        sl = c % 2
                        P.waits("tensor", [ev_l5[par], ev_w5, ev_w5b, ev_py_free[sl]])
                        evm = None
                        for hc in range(24):
                            P.emit("tensor", lambda e, hc=hc, c=c, sl=sl, par=par:
                                   e.matmul(py[sl][:, 0:512],
                                            h5[par][:, hc * 512 + c * 128:
                                                    hc * 512 + (c + 1) * 128],
                                            wd_b[:, hc * 1024:hc * 1024 + 512],
                                            start=(hc == 0), stop=(hc == 23)), sem="mm")
                            evm = P.emit("tensor", lambda e, hc=hc, c=c, sl=sl, par=par:
                                         e.matmul(py[sl][:, 512:1024],
                                                  h5[par][:, hc * 512 + c * 128:
                                                          hc * 512 + (c + 1) * 128],
                                                  wd_b[:, hc * 1024 + 512:
                                                       (hc + 1) * 1024],
                                                  start=(hc == 0), stop=(hc == 23)),
                                         sem="mm")
                        ev_mm5 = evm
                        cst = c % 4
                        ev_y = P.emit("scalar", lambda e, sl=sl: e.activation(
                            y_sb[:], py[sl][:], AF.Identity, scale=QSCALE), sem="act",
                            waits=[evm, ev_y_free], writes=("y5",))
                        ev_py_free[sl] = ev_y
                        r5, n5, evn5, _ = ln_stats(12 + cst, y_sb[:], ev_y, 1.0, EPS,
                                                   xtok="y5")
                        ev_t5 = P.emit("vector", lambda e, r5=r5, n5=n5: e.tensor_scalar(
                            t5_sb[:], y_sb[:], r5, n5, ALU.mult, ALU.add), sem="dve",
                            waits=[evn5],
                            reads=("y5", f"r{12 + cst}", f"n{12 + cst}"), writes=("t5",))
                        ev_y_free = ev_t5
                        x1ap = x15[par][:, c * 1024:(c + 1) * 1024]
                        ev_o = P.emit("vector", lambda e, x1ap=x1ap: e.tensor_add(
                            x1ap, x1ap, t5_sb[:]), sem="dve", reads=("t5",))
                        P.wait("sync", ev_o)
                        ev_ost = P.emit("sync", lambda e, t=t, c=c, x1ap=x1ap:
                                        e.dma_start(out=out_d[t * 512 + c * 128:
                                                              t * 512 + (c + 1) * 128, :],
                                                    in_=x1ap), sem="so", amount=16)
                    ev_b5_free[par] = [ev_mm5, ev_ost]
                P.wait("sync", ev_ost)
            P.run_block()
        else:
            pass

    return nc


_CACHE = {}


def kernel(L0_attn_x, L0_attn_w_qkv, L0_attn_b_qkv, L0_attn_w_gate, L0_attn_b_gate,
           L0_attn_w_proj, L0_attn_b_proj, L0_attn_ln_g, L0_attn_ln_b,
           L0_ffn_w_gate, L0_ffn_w_up, L0_ffn_w_down):
    x = np.asarray(L0_attn_x, dtype=np.float32)
    B, T, _ = x.shape
    w_qkv = np.ascontiguousarray(np.asarray(L0_attn_w_qkv, dtype=np.float32))
    w_gate = np.ascontiguousarray(np.asarray(L0_attn_w_gate, dtype=np.float32))
    w_proj = np.ascontiguousarray(np.asarray(L0_attn_w_proj, dtype=np.float32))
    b_qkv = np.asarray(L0_attn_b_qkv, dtype=np.float32)
    b_gate = np.asarray(L0_attn_b_gate, dtype=np.float32)
    b_proj = np.asarray(L0_attn_b_proj, dtype=np.float32)
    ln_g = np.asarray(L0_attn_ln_g, dtype=np.float32)
    ln_b = np.asarray(L0_attn_ln_b, dtype=np.float32)
    wg_t = np.ascontiguousarray(np.asarray(L0_ffn_w_gate, dtype=np.int32).T)
    wu_t = np.ascontiguousarray(np.asarray(L0_ffn_w_up, dtype=np.int32).T)
    wd_t = np.ascontiguousarray(np.asarray(L0_ffn_w_down, dtype=np.int32).T)

    b_qkv_fm = np.ascontiguousarray(b_qkv.reshape(24, 128).T)
    b_gate_fm = np.ascontiguousarray(b_gate.reshape(8, 128).T)
    ln_g_fm = np.ascontiguousarray(ln_g.reshape(8, 128).T)
    ln_b_fm = np.ascontiguousarray(ln_b.reshape(8, 128).T)
    b_proj_t = np.ascontiguousarray(np.broadcast_to(b_proj, (128, D)))
    b_v_t = np.ascontiguousarray(np.broadcast_to(b_qkv[2 * D:3 * D], (128, D)))

    if T not in _CACHE:
        _CACHE[T] = build_nc(T)
    nc = _CACHE[T]
    shared = dict(w_qkv=w_qkv, w_gate=w_gate, w_proj=w_proj, b_qkv_fm=b_qkv_fm,
                  b_gate_fm=b_gate_fm, b_proj_t=b_proj_t, b_v_t=b_v_t,
                  ln_g_fm=ln_g_fm, ln_b_fm=ln_b_fm, wg_t=wg_t, wu_t=wu_t, wd_t=wd_t)
    in_maps = [dict(shared, x=np.ascontiguousarray(x[i])) for i in range(B)]
    res = run_bass_kernel_spmd(nc, in_maps, core_ids=list(range(B)))
    return np.stack([np.asarray(res.results[i]["out"]) for i in range(B)])
